# version 04
# baseline (speedup 1.0000x reference)
import contextlib
import numpy as np
import ml_dtypes
import concourse.bass as bass
import concourse.mybir as mybir
from concourse.bass_utils import run_bass_kernel_spmd

F32 = mybir.dt.float32
BF16 = mybir.dt.bfloat16
AF = mybir.ActivationFunctionType
ALU = mybir.AluOpType
AX = mybir.AxisListType

H, DI, T, L, NS, KTAP = 1024, 2048, 4096, 4, 16, 4
CK = 512
NT = T // CK
NCH = H // 128
NDC = DI // 128
EXP, FF = 8, 2048
NFC = FF // 128
EPS = 1e-5
NCORE = 8
BF = ml_dtypes.bfloat16
ARENA_COLS = 49800


def _col_layout():
    lay, off = {}, [0]

    def add(name, n):
        lay[name] = off[0]
        off[0] += n

    add("inng", NCH); add("innb", NCH)
    add("lng", L * NCH); add("lnb", L * NCH)
    add("convw", L * 2 * KTAP * NDC); add("convb", L * 2 * NDC)
    add("dtw", L * NDC); add("dtb", L * NDC)
    add("Dd", L * NDC); add("mlg", L * NDC); add("mlb", L * NDC)
    add("op1b", NCH); add("op2b", NCH)
    add("b1c", EXP * NFC)
    add("gam", NCH); add("bet", NCH)
    return lay, off[0]


COLL, NCOL = _col_layout()


def _cols(v):
    v = np.asarray(v, np.float32)
    lead, C = v.shape[:-1], v.shape[-1]
    r = v.reshape(*lead, C // 128, 128)
    r = np.moveaxis(r, -1, 0)
    return np.ascontiguousarray(r.reshape(128, -1))


class Arena:
    def __init__(self, n):
        self.n, self.off = n, 0

    def alloc(self, cols):
        cols = (cols + 7) & ~7
        o = self.off
        self.off += cols
        assert self.off <= self.n, f"arena overflow {self.off} > {self.n}"
        return o


def build_nc():
    nc = bass.Bass("TRN2", target_bir_lowering=False)

    x0t = nc.dram_tensor("x0t", [H, T], F32, kind="ExternalInput").ap()
    wi = nc.dram_tensor("wi", [L, H, 2 * DI], F32, kind="ExternalInput").ap()
    wo = nc.dram_tensor("wo", [L, DI, H], F32, kind="ExternalInput").ap()
    wx = nc.dram_tensor("wx", [L, DI, 128], F32, kind="ExternalInput").ap()
    o1w = nc.dram_tensor("o1w", [H, H], F32, kind="ExternalInput").ap()
    o2w = nc.dram_tensor("o2w", [H, H], F32, kind="ExternalInput").ap()
    w1i = nc.dram_tensor("w1i", [EXP, H, FF], F32, kind="ExternalInput").ap()
    w2i = nc.dram_tensor("w2i", [EXP, FF, H], F32, kind="ExternalInput").ap()
    wgi = nc.dram_tensor("wgi", [H, EXP], F32, kind="ExternalInput").ap()
    b2r = nc.dram_tensor("b2r", [EXP, H], F32, kind="ExternalInput").ap()
    colp = nc.dram_tensor("colp", [128, NCOL], F32, kind="ExternalInput").ap()
    bgr = nc.dram_tensor("bgr", [128, EXP], F32, kind="ExternalInput").ap()
    idb = nc.dram_tensor("idb", [128, 128], BF16, kind="ExternalInput").ap()
    cstv = nc.dram_tensor("cstv", [128, 24], F32, kind="ExternalInput").ap()

    wib = nc.dram_tensor("wib", [L, H, 2 * DI], BF16, kind="Internal").ap()
    wob = nc.dram_tensor("wob", [L, DI, H], BF16, kind="Internal").ap()
    w1b = nc.dram_tensor("w1b", [EXP, H, FF], BF16, kind="Internal").ap()
    w2b = nc.dram_tensor("w2b", [EXP, FF, H], BF16, kind="Internal").ap()
    wmb = nc.dram_tensor("wmb", [2, H, H], BF16, kind="Internal").ap()
    xs = nc.dram_tensor("xs", [H, T], F32, kind="Internal").ap()
    ydr = nc.dram_tensor("ydr", [DI, CK], BF16, kind="Internal").ap()
    sgd = nc.dram_tensor("sgd", [DI, CK], BF16, kind="Internal").ap()
    hed = nc.dram_tensor("hed", [EXP, FF, CK], BF16, kind="Internal").ap()
    rwf = nc.dram_tensor("rwf", [8, CK], F32, kind="Internal").ap()
    rwb = nc.dram_tensor("rwb", [48, CK], BF16, kind="Internal").ap()

    outt = nc.dram_tensor("outt", [H, T], F32, kind="ExternalOutput").ap()

    ctx = contextlib.ExitStack()
    with ctx:
        arena_t = ctx.enter_context(nc.sbuf_tensor("arena", [128, ARENA_COLS], F32))
        colsb_t = ctx.enter_context(nc.sbuf_tensor("colsb", [128, NCOL], F32))
        bg_t = ctx.enter_context(nc.sbuf_tensor("bg", [128, EXP], F32))
        id_t = ctx.enter_context(nc.sbuf_tensor("identb", [128, 128], BF16))
        one_t = ctx.enter_context(nc.sbuf_tensor("onecol", [128, 1], BF16))
        hst_t = ctx.enter_context(nc.sbuf_tensor("hstate", [128, NDC * NS], F32))
        wgb_t = ctx.enter_context(nc.sbuf_tensor("wgb", [128, NCH * EXP], BF16))
        b2b_t = ctx.enter_context(nc.sbuf_tensor("b2b", [EXP, H], BF16))
        wxb_t = ctx.enter_context(nc.sbuf_tensor("wxb", [128, NDC * 128], BF16))
        wfm_t = ctx.enter_context(nc.sbuf_tensor("wfm", [EXP, CK], BF16))
        cst_t = ctx.enter_context(nc.sbuf_tensor("cst", [128, 24], F32))

        pm = [ctx.enter_context(nc.psum_tensor(f"pm{i}", [128, 512], F32))
              for i in range(4)]
        ps1 = ctx.enter_context(nc.psum_tensor("ps1", [1, 512], F32))
        ps2 = ctx.enter_context(nc.psum_tensor("ps2", [1, 512], F32))
        plg = ctx.enter_context(nc.psum_tensor("plg", [128, 32], F32))
        ptp = ctx.enter_context(nc.psum_tensor("ptp", [128, 128], BF16))

        SEM = {s: ctx.enter_context(nc.semaphore(s))
               for s in ("q1", "q2", "sp", "sa", "sv")}

        arena = arena_t.ap()
        arena_bf = arena.bitcast(BF16)
        colsb = colsb_t.ap()
        bg_sb = bg_t.ap()
        ident = id_t.ap()
        onec = one_t.ap()
        hstate = hst_t.ap()
        wgbf = wgb_t.ap()
        b2bf = b2b_t.ap()
        wxbf = wxb_t.ap()
        wfm = wfm_t.ap()
        cst = cst_t.ap()
        _cvals = [EPS, -0.5, -1.0, 1.0] + [-float(n) for n in range(2, 17)]
        for ci_, cv_ in enumerate(_cvals):
            nc.const_aps.register(F32, cv_, cst[:, ci_:ci_ + 1])

        def fA(off, n, p0=0, p1=128):
            return arena[p0:p1, off:off + n]

        def bA(off_f32, nbf, p0=0, p1=128):
            return arena_bf[p0:p1, 2 * off_f32:2 * off_f32 + nbf]

        def ccol(name, idx):
            return colsb[:, COLL[name] + idx:COLL[name] + idx + 1]

        am = Arena(ARENA_COLS)
        o_xch = am.alloc(NCH * CK)
        o_abf = am.alloc(4 * CK)
        o_xsq = am.alloc(NCH * CK // 2)
        o_xnb = o_xsq
        o_R1 = am.alloc(NDC * 520)
        o_xiC = am.alloc(NDC * CK // 2)
        o_sgR = am.alloc(4 * CK // 2)
        o_pjB = am.alloc(CK // 2)
        o_pjC = am.alloc(CK // 2)
        o_Bre = am.alloc(NS * CK // 2)
        o_Cre = am.alloc(NS * CK // 2)
        o_rre = am.alloc(CK)
        o_mur = am.alloc(CK)
        o_rsr = am.alloc(CK)
        o_ab = am.alloc(CK + CK // 2)
        o_dts = am.alloc(2 * 1536)
        o_ydc = am.alloc(2 * CK // 2)
        o_ysq = am.alloc(2 * CK // 2)
        o_dxc = am.alloc(CK // 2)
        o_dia = am.alloc(2 * NDC * KTAP * 64)
        o_wsl = am.alloc(8 * CK // 2)
        o_wos = am.alloc(2 * CK // 2)
        o_tlA = am.alloc(NDC * 4)
        o_tlB = am.alloc(NDC * 4)
        o_row = am.alloc(2 * CK)
        o_lnt = am.alloc(CK)
        M_TOP = am.off

        ap_ = Arena(ARENA_COLS)
        p_xch = ap_.alloc(NCH * CK)
        p_xbf = ap_.alloc(NCH * CK // 2)
        p_g1 = ap_.alloc(NCH * CK // 2)
        p_bq = None
        p_ub = ap_.alloc(NCH * CK)
        p_x0c = ap_.alloc(NCH * CK)
        p_ws = ap_.alloc(8 * CK // 2)
        p_sq = ap_.alloc(NCH * CK // 2)
        p_mur = ap_.alloc(CK)
        p_rsr = ap_.alloc(CK)
        p_row = ap_.alloc(2 * CK)
        p_lnt = ap_.alloc(CK)

        ae = Arena(ARENA_COLS)
        e_x1 = ae.alloc(NCH * CK)
        e_x1b = ae.alloc(NCH * CK // 2)
        e_m2 = ae.alloc(NCH * CK)
        e_w1s = ae.alloc(NCH * FF // 2)
        e_w2s = ae.alloc(2 * CK // 2)
        e_wer = ae.alloc(2 * CK // 2)
        e_rlb = ae.alloc(2 * CK // 2)
        e_hrt = ae.alloc(4 * CK // 2)
        e_t2 = ae.alloc(4 * 64)
        e_sq = ae.alloc(NCH * CK // 2)
        e_mur = ae.alloc(CK)
        e_rsr = ae.alloc(CK)
        e_row = ae.alloc(2 * CK)
        e_lnt = ae.alloc(CK)
        e_fs = ae.alloc(CK)

        ac = Arena(ARENA_COLS)
        c_f = [ac.alloc(2048) for _ in range(2)]
        c_b = [ac.alloc(1024) for _ in range(2)]

        class Em:
            def __init__(self, want, eng):
                self.want, self.eng = want, eng
                self.c = {k: 0 for k in SEM}

            def _go(self, key, fn, inc, n):
                if self.want == key:
                    ins = fn(self.eng)
                    if inc is not None:
                        ins.then_inc(SEM[inc], n)
                if inc is not None:
                    self.c[inc] += n

            def t(self, fn, inc=None):
                self._go("t", fn, inc, 1)

            def v(self, fn, inc=None):
                self._go("v", fn, inc, 1)

            def s(self, fn, inc=None):
                self._go("s", fn, inc, 1)

            def q1(self, fn):
                self._go("q1", fn, "q1", 16)

            def q2(self, fn):
                self._go("q2", fn, "q2", 16)

            def w(self, key, *sems):
                if self.want == key:
                    for s_ in sems:
                        self.eng.wait_ge(SEM[s_], self.c[s_])

            def wv(self, key, sem, val):
                if self.want == key:
                    self.eng.wait_ge(SEM[sem], val)

            def barrier(self):
                for k in ("t", "v", "s", "q1", "q2"):
                    self.w(k, "q1", "q2", "sp", "sa", "sv")

        def sched(em: Em):
            invH = 1.0 / H
            invDI = 1.0 / DI

            em.q1(lambda e: e.dma_start(out=cst[:, :], in_=cstv[:, :]))
            em.q1(lambda e: e.dma_start(out=colsb[:, :], in_=colp[:, :]))
            em.q1(lambda e: e.dma_start(out=bg_sb[:, :], in_=bgr[:, :]))
            em.q1(lambda e: e.dma_start(out=ident[:, :], in_=idb[:, :]))
            em.w("v", "q1")
            em.v(lambda e: e.memset(onec[:, :], 1.0))
            em.v(lambda e: e.memset(hstate[:, :], 0.0), inc="sv")

            for k in range(NCH):
                em.wv("q1", "sv", em.c["sv"])
                em.q1(lambda e, k=k: e.dma_start(
                    out=fA(c_f[0], EXP), in_=wgi[k * 128:(k + 1) * 128, :]))
                em.w("v", "q1")
                em.v(lambda e, k=k: e.tensor_copy(
                    wgbf[:, k * EXP:(k + 1) * EXP], fA(c_f[0], EXP)), inc="sv")
            em.w("q1", "sv")
            em.q1(lambda e: e.dma_start(out=fA(c_f[1], 1024, 0, 8), in_=b2r[:, :]))
            em.w("v", "q1")
            em.v(lambda e: e.tensor_copy(b2bf[:, :], fA(c_f[1], 1024, 0, 8)),
                 inc="sv")

            jobs = []
            for l in range(L):
                for r in range(NCH):
                    for c2 in range(2):
                        jobs.append((wi[l, r * 128:(r + 1) * 128,
                                        c2 * 2048:(c2 + 1) * 2048],
                                     wib[l, r * 128:(r + 1) * 128,
                                         c2 * 2048:(c2 + 1) * 2048], 2048))
                for r in range(NDC):
                    jobs.append((wo[l, r * 128:(r + 1) * 128, :],
                                 wob[l, r * 128:(r + 1) * 128, :], 1024))
            for e8 in range(EXP):
                for r in range(NCH):
                    jobs.append((w1i[e8, r * 128:(r + 1) * 128, :],
                                 w1b[e8, r * 128:(r + 1) * 128, :], 2048))
                for r in range(NFC):
                    jobs.append((w2i[e8, r * 128:(r + 1) * 128, :],
                                 w2b[e8, r * 128:(r + 1) * 128, :], 1024))
            for m, wsrc in ((0, o1w), (1, o2w)):
                for r in range(NCH):
                    jobs.append((wsrc[r * 128:(r + 1) * 128, :],
                                 wmb[m, r * 128:(r + 1) * 128, :], 1024))

            ld_ms, cp_ms, st_ms = [], [], []
            for i, (src, dst, n) in enumerate(jobs):
                p = i % 2
                if i >= 2:
                    em.wv("q1", "sv", cp_ms[i - 2])
                em.q1(lambda e, src=src, p=p, n=n: e.dma_start(
                    out=fA(c_f[p], n), in_=src))
                ld_ms.append(em.c["q1"])
                if i >= 2:
                    em.wv("v", "q1", st_ms[i - 2])
                em.wv("v", "q1", ld_ms[i])
                em.v(lambda e, p=p, n=n: e.tensor_copy(
                    bA(c_b[p], n), fA(c_f[p], n)), inc="sv")
                cp_ms.append(em.c["sv"])
                em.wv("q1", "sv", cp_ms[i])
                em.q1(lambda e, dst=dst, p=p, n=n: e.dma_start(
                    out=dst, in_=bA(c_b[p], n)))
                st_ms.append(em.c["q1"])
            em.barrier()

            def row_stats_close(o_murep, o_rsrep, invN, ro, ro2):
                r0 = fA(ro + 0, CK, 0, 1)
                r1 = fA(ro + CK, CK, 0, 1)
                r2 = fA(ro2, CK, 0, 1)
                em.w("v", "sp", "q1")
                em.v(lambda e: e.tensor_scalar(r0, ps1.ap()[0:1, :], invN, None,
                                               ALU.mult))
                em.v(lambda e: e.tensor_scalar(r1, ps2.ap()[0:1, :], invN, None,
                                               ALU.mult))
                em.v(lambda e: e.tensor_mul(r2, r0, r0))
                em.v(lambda e: e.tensor_sub(r1, r1, r2), inc="sv")
                em.w("s", "sv")
                em.s(lambda e: e.activation(r2, r1, AF.Ln, bias=EPS))
                em.s(lambda e: e.activation(r1, r2, AF.Exp, scale=-0.5), inc="sa")
                em.w("q1", "sa")
                em.q1(lambda e: e.dma_start(out=rwf[1:2, :], in_=r0))
                em.q1(lambda e: e.dma_start(out=rwf[2:3, :], in_=r1))
                em.w("q2", "q1")
                em.q2(lambda e: e.dma_start(
                    out=fA(o_murep, CK),
                    in_=rwf[1:2, :].partition_broadcast(128)))
                em.q2(lambda e: e.dma_start(
                    out=fA(o_rsrep, CK),
                    in_=rwf[2:3, :].partition_broadcast(128)))

            def ln_stats(o_src, o_sq, nch):
                em.w("s", "q1", "sv")
                for k in range(nch):
                    em.s(lambda e, k=k: e.activation(
                        bA(o_sq + k * CK // 2, CK), fA(o_src + k * CK, CK),
                        AF.Square), inc="sa" if k == nch - 1 else None)
                em.w("t", "sv", "sa", "q1")
                for k in range(nch):
                    em.t(lambda e, k=k: e.matmul(
                        ps1.ap()[0:1, :], cst[:, 3:4], fA(o_src + k * CK, CK),
                        start=(k == 0), stop=(k == nch - 1)))
                    em.t(lambda e, k=k: e.matmul(
                        ps2.ap()[0:1, :], onec[:, :], bA(o_sq + k * CK // 2, CK),
                        start=(k == 0), stop=(k == nch - 1)),
                        inc="sp" if k == nch - 1 else None)

            for j in range(NT):
                sl = slice(j * CK, (j + 1) * CK)
                em.barrier()
                for k in range(NCH):
                    em.q1(lambda e, k=k, sl=sl: e.dma_start(
                        out=fA(o_xch + k * CK, CK),
                        in_=x0t[k * 128:(k + 1) * 128, sl]))
                ln_stats(o_xch, o_xsq, NCH)
                row_stats_close(o_mur, o_rsr, invH, o_row, o_lnt)
                em.w("v", "q2")
                for k in range(NCH):
                    em.v(lambda e, k=k: e.tensor_sub(
                        fA(o_lnt, CK), fA(o_xch + k * CK, CK), fA(o_mur, CK)))
                    em.v(lambda e, k=k: e.scalar_tensor_tensor(
                        fA(o_xch + k * CK, CK), fA(o_lnt, CK), ccol("inng", k),
                        fA(o_rsr, CK), ALU.mult, ALU.mult))
                    em.v(lambda e, k=k: e.tensor_scalar(
                        fA(o_xch + k * CK, CK), fA(o_xch + k * CK, CK),
                        ccol("innb", k), None, ALU.add),
                        inc="sv" if k == NCH - 1 else None)
                em.w("q1", "sv")
                for k in range(NCH):
                    em.q1(lambda e, k=k, sl=sl: e.dma_start(
                        out=xs[k * 128:(k + 1) * 128, sl],
                        in_=fA(o_xch + k * CK, CK)))
            em.barrier()

            for l in range(L):
                em.v(lambda e: e.memset(hstate[:, :], 0.0))
                for dc in range(NDC):
                    em.wv("q1", "sv", em.c["sv"])
                    em.q1(lambda e, l=l, dc=dc: e.dma_start(
                        out=fA(o_ab, 128), in_=wx[l, dc * 128:(dc + 1) * 128, :]))
                    em.w("v", "q1")
                    em.v(lambda e, dc=dc: e.tensor_copy(
                        wxbf[:, dc * 128:(dc + 1) * 128], fA(o_ab, 128)),
                        inc="sv")
                for c2 in range(2):
                    for dc in range(NDC):
                        for k in range(KTAP):
                            di = ((c2 * NDC + dc) * KTAP + k) * 64
                            ci = ((l * 2 + c2) * KTAP + k) * NDC + dc
                            em.v(lambda e, di=di, ci=ci: e.tensor_scalar(
                                bA(o_dia + di, 128), ident[:, :],
                                ccol("convw", ci), None, ALU.mult))
                em.v(lambda e: e.memset(bA(o_tlA, 4 * NDC), 0.0))
                em.v(lambda e: e.memset(bA(o_tlB, 4 * NDC), 0.0), inc="sv")
                em.barrier()

                for j in range(NT):
                    sl = slice(j * CK, (j + 1) * CK)
                    em.barrier()
                    for k in range(NCH):
                        em.q1(lambda e, k=k, sl=sl: e.dma_start(
                            out=fA(o_xch + k * CK, CK),
                            in_=xs[k * 128:(k + 1) * 128, sl]))
                    ln_stats(o_xch, o_xsq, NCH)
                    row_stats_close(o_mur, o_rsr, invH, o_row, o_lnt)
                    em.w("v", "q2", "sp")
                    for k in range(NCH):
                        em.v(lambda e, k=k: e.tensor_sub(
                            fA(o_lnt, CK), fA(o_xch + k * CK, CK), fA(o_mur, CK)))
                        em.v(lambda e, k=k, l=l: e.scalar_tensor_tensor(
                            bA(o_xnb + k * CK // 2, CK), fA(o_lnt, CK),
                            ccol("lng", l * NCH + k), fA(o_rsr, CK),
                            ALU.mult, ALU.mult))
                        em.v(lambda e, k=k, l=l: e.tensor_scalar(
                            bA(o_xnb + k * CK // 2, CK),
                            bA(o_xnb + k * CK // 2, CK),
                            ccol("lnb", l * NCH + k), None, ALU.add),
                            inc="sv" if k == NCH - 1 else None)
                    for dc in range(NDC):
                        em.v(lambda e, dc=dc: e.tensor_copy(
                            bA(o_R1 + dc * 260, 520)[:, 0:3],
                            bA(o_tlA + dc * 2, 3)))
                        em.v(lambda e, dc=dc: e.tensor_copy(
                            bA(o_R1 + (NDC + dc) * 260, 520)[:, 0:3],
                            bA(o_tlB + dc * 2, 3)),
                            inc="sv" if dc == NDC - 1 else None)

                    evac_ms = {}
                    sg_dma = {}
                    mm_ms = {}
                    for og in range(8):
                        if og >= 1:
                            em.wv("q1", "sp", mm_ms[og - 1])
                        for k in range(NCH):
                            em.q1(lambda e, l=l, k=k, og=og: e.dma_start(
                                out=bA(o_wsl + k * CK // 2, CK),
                                in_=wib[l, k * 128:(k + 1) * 128,
                                        og * 512:(og + 1) * 512]))
                        ld = em.c["q1"]
                        em.wv("t", "q1", ld)
                        em.w("t", "sv")
                        for oi in range(4):
                            o = og * 4 + oi
                            if o >= 4:
                                em.wv("t", "sa", evac_ms.get(oi, 0))
                            for k in range(NCH):
                                em.t(lambda e, k=k, oi=oi: e.matmul(
                                    pm[oi].ap()[:, :],
                                    bA(o_wsl + k * CK // 2, CK)[
                                        :, oi * 128:(oi + 1) * 128],
                                    bA(o_xnb + k * CK // 2, CK),
                                    start=(k == 0), stop=(k == NCH - 1)),
                                    inc="sp" if k == NCH - 1 else None)
                            if oi == 3:
                                mm_ms[og] = em.c["sp"]
                            em.wv("s", "sp", em.c["sp"])
                            if o < NDC:
                                em.s(lambda e, o=o, oi=oi: e.activation(
                                    bA(o_R1 + o * 260, 520)[:, 3:3 + CK],
                                    pm[oi].ap()[:, :], AF.Silu), inc="sa")
                            else:
                                gp2 = (o - NDC) % 2
                                if o - NDC >= 2:
                                    em.wv("s", "q1", sg_dma[gp2])
                                em.s(lambda e, gp2=gp2, oi=oi: e.activation(
                                    bA(o_sgR + gp2 * CK // 2, CK),
                                    pm[oi].ap()[:, :], AF.Sigmoid), inc="sa")
                                em.wv("q1", "sa", em.c["sa"])
                                em.q1(lambda e, o=o, gp2=gp2: e.dma_start(
                                    out=sgd[(o - NDC) * 128:(o - NDC + 1) * 128, :],
                                    in_=bA(o_sgR + gp2 * CK // 2, CK)))
                                sg_dma[gp2] = em.c["q1"]
                            evac_ms[oi] = em.c["sa"]

                    for c2 in range(2):
                        em.w("t", "sa", "sv")
                        cv_ms = {}
                        for dc in range(NDC):
                            oi = dc % 2
                            if dc >= 2:
                                em.wv("t", "sa", cv_ms[oi])
                            src = o_R1 + (c2 * NDC + dc) * 260
                            for k in range(KTAP):
                                di = ((c2 * NDC + dc) * KTAP + k) * 64
                                em.t(lambda e, di=di, src=src, k=k, oi=oi:
                                     e.matmul(pm[oi].ap()[:, :],
                                              bA(o_dia + di, 128),
                                              bA(src, 520)[:, k:k + CK],
                                              start=(k == 0),
                                              stop=(k == KTAP - 1)),
                                     inc="sp" if k == KTAP - 1 else None)
                            em.wv("s", "sp", em.c["sp"])
                            ci = (l * 2 + c2) * NDC + dc
                            if c2 == 0:
                                dst = o_R1 + (NDC + dc) * 260
                                em.s(lambda e, dst=dst, oi=oi, ci=ci:
                                     e.activation(bA(dst, 520)[:, 3:3 + CK],
                                                  pm[oi].ap()[:, :], AF.Silu,
                                                  bias=ccol("convb", ci)),
                                     inc="sa")
                            else:
                                em.s(lambda e, dc=dc, oi=oi, ci=ci:
                                     e.activation(bA(o_xiC + dc * CK // 2, CK),
                                                  pm[oi].ap()[:, :], AF.Silu,
                                                  bias=ccol("convb", ci)),
                                     inc="sa")
                            cv_ms[oi] = em.c["sa"]
                        em.w("v", "sp", "sa")
                        tl = o_tlA if c2 == 0 else o_tlB
                        base = 0 if c2 == 0 else NDC
                        for dc in range(NDC):
                            em.v(lambda e, tl=tl, dc=dc, base=base: e.tensor_copy(
                                bA(tl + dc * 2, 3),
                                bA(o_R1 + (base + dc) * 260, 520)[:, CK:CK + 3]),
                                inc="sv" if dc == NDC - 1 else None)

                    em.w("t", "sa", "sv")
                    for dc in range(NDC):
                        em.t(lambda e, dc=dc: e.matmul(
                            pm[2].ap()[:, :], wxbf[:, dc * 128:(dc + 1) * 128],
                            bA(o_xiC + dc * CK // 2, CK),
                            start=(dc == 0), stop=(dc == NDC - 1)),
                            inc="sp" if dc == NDC - 1 else None)
                    em.w("v", "sp", "q1")
                    em.v(lambda e: e.tensor_copy(fA(o_row, CK, 0, 1),
                                                 pm[2].ap()[0:1, :]))
                    em.v(lambda e: e.tensor_copy(bA(o_pjB, CK, 0, 16),
                                                 pm[2].ap()[32:48, :]))
                    em.v(lambda e: e.tensor_copy(bA(o_pjC, CK, 0, 16),
                                                 pm[2].ap()[64:80, :]), inc="sv")
                    em.w("q1", "sv")
                    em.q1(lambda e: e.dma_start(out=rwf[0:1, :],
                                                in_=fA(o_row, CK, 0, 1)))
                    em.q1(lambda e: e.dma_start(out=rwb[0:16, :],
                                                in_=bA(o_pjB, CK, 0, 16)))
                    em.q1(lambda e: e.dma_start(out=rwb[16:32, :],
                                                in_=bA(o_pjC, CK, 0, 16)))
                    em.w("q2", "q1")
                    em.q2(lambda e: e.dma_start(
                        out=fA(o_rre, CK),
                        in_=rwf[0:1, :].partition_broadcast(128)))
                    for n in range(NS):
                        em.q2(lambda e, n=n: e.dma_start(
                            out=bA(o_Bre + n * CK // 2, CK),
                            in_=rwb[n:n + 1, :].partition_broadcast(128)))
                        em.q2(lambda e, n=n: e.dma_start(
                            out=bA(o_Cre + n * CK // 2, CK),
                            in_=rwb[16 + n:17 + n, :].partition_broadcast(128)))

                    em.w("s", "q2")
                    em.w("v", "q2")
                    vms, sams, stat_ms, park_ms = {}, {}, {}, {}
                    slot_ms = {}
                    for dc in range(NDC):
                        li = l * NDC + dc
                        p = dc % 2
                        dt_p = o_dts + p * 1536
                        d1_p = dt_p + 512
                        d2_p = dt_p + 1024
                        if dc >= 2:
                            em.wv("s", "sv", vms[dc - 2])
                        em.s(lambda e, li=li, dt_p=dt_p: e.activation(
                            fA(dt_p, CK), fA(o_rre, CK), AF.Identity,
                            bias=ccol("dtb", li), scale=ccol("dtw", li)))
                        em.s(lambda e, dt_p=dt_p, d1_p=d1_p: e.activation(
                            fA(d1_p, CK), fA(dt_p, CK), AF.Abs))
                        em.s(lambda e, d1_p=d1_p, d2_p=d2_p: e.activation(
                            fA(d2_p, CK), fA(d1_p, CK), AF.Exp, scale=-1.0))
                        em.s(lambda e, d1_p=d1_p, d2_p=d2_p: e.activation(
                            fA(d1_p, CK), fA(d2_p, CK), AF.Ln, bias=1.0))
                        em.s(lambda e: e.activation(
                            cst[:, 20:21], cst[:, 20:21], AF.Identity),
                            inc="sa")
                        sams[dc] = em.c["sa"]
                        em.wv("v", "sa", sams[dc])
                        if dc >= 2:
                            em.wv("v", "sp", stat_ms[dc - 2])
                            em.wv("v", "q1", park_ms[dc - 2])
                        em.v(lambda e, dt_p=dt_p, d2_p=d2_p: e.tensor_scalar(
                            fA(d2_p, CK), fA(dt_p, CK), 0.0, None, ALU.max))
                        em.v(lambda e, dt_p=dt_p, d1_p=d1_p, d2_p=d2_p:
                             e.tensor_add(fA(dt_p, CK), fA(d2_p, CK),
                                          fA(d1_p, CK)))
                        em.v(lambda e, dc=dc, dt_p=dt_p: e.tensor_mul(
                            bA(o_dxc, CK), fA(dt_p, CK),
                            bA(o_xiC + dc * CK // 2, CK)), inc="sv")
                        dt_ready = em.c["sv"]
                        em.wv("s", "sv", dt_ready)
                        for g in range(NS // 2):
                            for i in range(2):
                                n = 2 * g + i
                                em.wv("s", "sv", slot_ms.get(n % 4, 0))
                                em.s(lambda e, n=n, dt_p=dt_p: e.activation(
                                    fA(o_abf + (n % 4) * CK, CK), fA(dt_p, CK),
                                    AF.Identity, bias=1.0,
                                    scale=-float(n + 1)),
                                    inc="sa" if i == 1 else None)
                            ag = em.c["sa"]
                            em.wv("v", "sa", ag)
                            for i in range(2):
                                n = 2 * g + i
                                em.v(lambda e, n=n: e.tensor_mul(
                                    bA(o_ab + CK, CK), bA(o_dxc, CK),
                                    bA(o_Bre + n * CK // 2, CK)))
                                em.v(lambda e, n=n, dc=dc:
                                     e.tensor_tensor_scan(
                                         bA(o_R1 + n * 260, CK),
                                         fA(o_abf + (n % 4) * CK, CK),
                                         bA(o_ab + CK, CK),
                                         hstate[:, dc * NS + n:dc * NS + n + 1],
                                         ALU.mult, ALU.add),
                                     inc="sv" if i == 1 else None)
                            slot_ms[(2 * g) % 4] = em.c["sv"]
                            slot_ms[(2 * g + 1) % 4] = em.c["sv"]
                        em.v(lambda e, dt_p=dt_p: e.tensor_copy(
                            fA(o_lnt, 64), fA(dt_p, 64)))
                        em.v(lambda e, dc=dc: e.tensor_copy(
                            hstate[:, dc * NS:(dc + 1) * NS],
                            bA(o_R1, NS * 520).rearrange(
                                "p (n t) -> p n t", t=520)[:, :, CK - 1]))
                        for n in range(NS):
                            em.v(lambda e, n=n: e.tensor_mul(
                                bA(o_R1 + n * 260, CK), bA(o_R1 + n * 260, CK),
                                bA(o_Cre + n * CK // 2, CK)))
                        step = NS
                        while step > 2:
                            half = step // 2
                            for i in range(half):
                                em.v(lambda e, i=i, half=half: e.tensor_add(
                                    bA(o_R1 + i * 260, CK),
                                    bA(o_R1 + i * 260, CK),
                                    bA(o_R1 + (i + half) * 260, CK)))
                            step = half
                        yb = o_ydc + p * CK // 2
                        em.v(lambda e, yb=yb: e.tensor_add(
                            bA(yb, CK), bA(o_R1, CK), bA(o_R1 + 260, CK)))
                        em.v(lambda e, dc=dc, li=li, yb=yb:
                             e.scalar_tensor_tensor(
                                 bA(yb, CK), bA(o_xiC + dc * CK // 2, CK),
                                 ccol("Dd", li), bA(yb, CK),
                                 ALU.mult, ALU.add))
                        em.v(lambda e, yb=yb, p=p: e.tensor_mul(
                            bA(o_ysq + p * CK // 2, CK), bA(yb, CK),
                            bA(yb, CK)), inc="sv")
                        vms[dc] = em.c["sv"]
                        em.wv("t", "sv", vms[dc])
                        em.t(lambda e, dc=dc, yb=yb: e.matmul(
                            ps1.ap()[0:1, :], onec[:, :], bA(yb, CK),
                            start=(dc == 0), stop=(dc == NDC - 1)))
                        em.t(lambda e, dc=dc, p=p: e.matmul(
                            ps2.ap()[0:1, :], onec[:, :],
                            bA(o_ysq + p * CK // 2, CK),
                            start=(dc == 0), stop=(dc == NDC - 1)), inc="sp")
                        stat_ms[dc] = em.c["sp"]
                        em.wv("q1", "sv", vms[dc])
                        em.q1(lambda e, dc=dc, yb=yb: e.dma_start(
                            out=ydr[dc * 128:(dc + 1) * 128, :],
                            in_=bA(yb, CK)))
                        park_ms[dc] = em.c["q1"]
                    row_stats_close(o_mur, o_rsr, invDI, o_row, o_lnt)

                    em.w("q1", "q1")
                    em.w("v", "q2")
                    app_ms = {}
                    for dc in range(NDC):
                        li = l * NDC + dc
                        p = dc % 2
                        yb = o_ydc + p * CK // 2
                        tb_ = o_dts + p * 256
                        if dc >= 2:
                            em.wv("q1", "sv", app_ms[dc - 2])
                        em.q1(lambda e, dc=dc, yb=yb: e.dma_start(
                            out=bA(yb, CK),
                            in_=ydr[dc * 128:(dc + 1) * 128, :]))
                        em.q1(lambda e, dc=dc, p=p: e.dma_start(
                            out=bA(o_sgR + (2 + p) * CK // 2, CK),
                            in_=sgd[dc * 128:(dc + 1) * 128, :]))
                        em.w("v", "q1")
                        em.v(lambda e, yb=yb: e.tensor_sub(
                            fA(o_lnt, CK), bA(yb, CK), fA(o_mur, CK)))
                        em.v(lambda e, li=li, tb_=tb_: e.scalar_tensor_tensor(
                            bA(tb_, CK), fA(o_lnt, CK), ccol("mlg", li),
                            fA(o_rsr, CK), ALU.mult, ALU.mult))
                        em.v(lambda e, li=li, tb_=tb_: e.tensor_scalar(
                            bA(tb_, CK), bA(tb_, CK), ccol("mlb", li),
                            None, ALU.add))
                        em.v(lambda e, dc=dc, p=p, tb_=tb_: e.tensor_mul(
                            bA(o_xiC + dc * CK // 2, CK), bA(tb_, CK),
                            bA(o_sgR + (2 + p) * CK // 2, CK)), inc="sv")
                        app_ms[dc] = em.c["sv"]
                    for hf in range(2):
                        em.w("t", "sv", "sa")
                        pe_ms = {}
                        for dc in range(NDC):
                            gp = dc % 2
                            if dc >= 2:
                                em.wv("q1", "sp", pe_ms[dc - 2])
                            em.q1(lambda e, l=l, dc=dc, hf=hf, gp=gp:
                                  e.dma_start(
                                      out=bA(o_wos + gp * CK // 2, CK),
                                      in_=wob[l, dc * 128:(dc + 1) * 128,
                                              hf * 512:(hf + 1) * 512]))
                            em.wv("t", "q1", em.c["q1"])
                            for oi in range(4):
                                em.t(lambda e, oi=oi, gp=gp, dc=dc: e.matmul(
                                    pm[oi].ap()[:, :],
                                    bA(o_wos + gp * CK // 2, CK)[
                                        :, oi * 128:(oi + 1) * 128],
                                    bA(o_xiC + dc * CK // 2, CK),
                                    start=(dc == 0), stop=(dc == NDC - 1)),
                                    inc="sp" if oi == 3 else None)
                            pe_ms[dc] = em.c["sp"]
                        em.w("v", "sp")
                        for oi in range(4):
                            o = hf * 4 + oi
                            em.v(lambda e, o=o, oi=oi: e.tensor_add(
                                fA(o_xch + o * CK, CK), fA(o_xch + o * CK, CK),
                                pm[oi].ap()[:, :]),
                                inc="sv" if oi == 3 else None)
                        em.w("q1", "sv")
                        for oi in range(4):
                            o = hf * 4 + oi
                            em.q1(lambda e, o=o, sl=sl: e.dma_start(
                                out=xs[o * 128:(o + 1) * 128, sl],
                                in_=fA(o_xch + o * CK, CK)))
                em.barrier()

            for j in range(NT):
                sl = slice(j * CK, (j + 1) * CK)
                em.barrier()
                for k in range(NCH):
                    em.q1(lambda e, k=k, sl=sl: e.dma_start(
                        out=fA(p_xch + k * CK, CK),
                        in_=xs[k * 128:(k + 1) * 128, sl]))
                    em.q1(lambda e, k=k, sl=sl: e.dma_start(
                        out=fA(p_x0c + k * CK, CK),
                        in_=x0t[k * 128:(k + 1) * 128, sl]))
                em.w("v", "q1")
                for k in range(NCH):
                    em.v(lambda e, k=k: e.tensor_copy(
                        bA(p_xbf + k * CK // 2, CK), fA(p_xch + k * CK, CK)),
                        inc="sv" if k == NCH - 1 else None)
                for mat in range(2):
                    ev_ms = {}
                    mm_ms = {}
                    for og in range(2):
                        if og >= 1 or mat == 1:
                            em.wv("q1", "sp", mm_ms.get(og - 1, em.c["sp"]))
                        for k in range(NCH):
                            em.q1(lambda e, k=k, og=og, mat=mat: e.dma_start(
                                out=bA(p_ws + k * CK // 2, CK),
                                in_=wmb[mat, k * 128:(k + 1) * 128,
                                        og * 512:(og + 1) * 512]))
                        em.wv("t", "q1", em.c["q1"])
                        em.w("t", "sv" if mat == 0 else "sa")
                        for oi in range(4):
                            o = og * 4 + oi
                            if o >= 4:
                                em.wv("t", "sa" if mat == 0 else "sv",
                                      ev_ms.get(oi, 0))
                            rhs_off = p_xbf if mat == 0 else p_g1
                            for k in range(NCH):
                                em.t(lambda e, k=k, oi=oi, rhs_off=rhs_off:
                                     e.matmul(
                                         pm[oi].ap()[:, :],
                                         bA(p_ws + k * CK // 2, CK)[
                                             :, oi * 128:(oi + 1) * 128],
                                         bA(rhs_off + k * CK // 2, CK),
                                         start=(k == 0), stop=(k == NCH - 1)),
                                     inc="sp" if k == NCH - 1 else None)
                            if oi == 3:
                                mm_ms[og] = em.c["sp"]
                            if mat == 0:
                                em.wv("s", "sp", em.c["sp"])
                                em.s(lambda e, o=o, oi=oi: e.activation(
                                    bA(p_g1 + o * CK // 2, CK),
                                    pm[oi].ap()[:, :], AF.Gelu,
                                    bias=ccol("op1b", o)), inc="sa")
                                ev_ms[oi] = em.c["sa"]
                            else:
                                em.wv("v", "sp", em.c["sp"])
                                em.v(lambda e, o=o, oi=oi:
                                     e.scalar_tensor_tensor(
                                         fA(p_ub + o * CK, CK),
                                         fA(p_x0c + o * CK, CK), 2.0,
                                         pm[oi].ap()[:, :], ALU.mult, ALU.add))
                                em.v(lambda e, o=o: e.tensor_scalar(
                                    fA(p_ub + o * CK, CK),
                                    fA(p_ub + o * CK, CK),
                                    ccol("op2b", o), None, ALU.add), inc="sv")
                                ev_ms[oi] = em.c["sv"]
                ln_stats(p_ub, p_sq, NCH)
                row_stats_close(p_mur, p_rsr, invH, p_row, p_lnt)
                em.w("v", "q2")
                for k in range(NCH):
                    em.v(lambda e, k=k: e.tensor_sub(
                        fA(p_lnt, CK), fA(p_ub + k * CK, CK), fA(p_mur, CK)))
                    em.v(lambda e, k=k: e.tensor_mul(
                        fA(p_ub + k * CK, CK), fA(p_lnt, CK), fA(p_rsr, CK)),
                        inc="sv" if k == NCH - 1 else None)
                em.w("q1", "sv")
                for k in range(NCH):
                    em.q1(lambda e, k=k, sl=sl: e.dma_start(
                        out=xs[k * 128:(k + 1) * 128, sl],
                        in_=fA(p_ub + k * CK, CK)))
            em.barrier()

            for j in range(NT):
                sl = slice(j * CK, (j + 1) * CK)
                em.barrier()
                for k in range(NCH):
                    em.q1(lambda e, k=k, sl=sl: e.dma_start(
                        out=fA(e_x1 + k * CK, CK),
                        in_=xs[k * 128:(k + 1) * 128, sl]))
                em.w("v", "q1")
                for k in range(NCH):
                    em.v(lambda e, k=k: e.tensor_copy(
                        bA(e_x1b + k * CK // 2, CK), fA(e_x1 + k * CK, CK)),
                        inc="sv" if k == NCH - 1 else None)
                TBS = range(4)

                def t2o(tb, off, n=1, bf=False):
                    base = e_t2 + tb * 64 + off
                    return bA(base, n) if bf else fA(base, n)

                em.w("t", "sv", "sa")
                for tb in TBS:
                    for k in range(NCH):
                        em.t(lambda e, k=k, tb=tb: e.matmul(
                            plg.ap()[:, tb * 8:(tb + 1) * 8],
                            bA(e_x1b + k * CK // 2, CK)[
                                :, tb * 128:(tb + 1) * 128],
                            wgbf[:, k * EXP:(k + 1) * EXP],
                            start=(k == 0), stop=(k == NCH - 1)),
                            inc="sp" if k == NCH - 1 else None)
                em.w("v", "sp")
                for tb in TBS:
                    em.v(lambda e, tb=tb: e.tensor_add(
                        t2o(tb, 0, EXP), bg_sb[:, :],
                        plg.ap()[:, tb * 8:(tb + 1) * 8]))
                for tb in TBS:
                    em.v(lambda e, tb=tb: e.reduce_max(
                        t2o(tb, 8), t2o(tb, 0, EXP), axis=AX.X))
                for tb in TBS:
                    em.v(lambda e, tb=tb: e.tensor_scalar(
                        t2o(tb, 40, EXP), t2o(tb, 0, EXP), t2o(tb, 8), None,
                        ALU.is_equal))
                for tb in TBS:
                    em.v(lambda e, tb=tb: e.scalar_tensor_tensor(
                        t2o(tb, 48, EXP), t2o(tb, 40, EXP), -1e30,
                        t2o(tb, 0, EXP), ALU.mult, ALU.add))
                for tb in TBS:
                    em.v(lambda e, tb=tb: e.reduce_max(
                        t2o(tb, 16), t2o(tb, 48, EXP), axis=AX.X))
                for tb in TBS:
                    em.v(lambda e, tb=tb: e.tensor_sub(
                        t2o(tb, 24), t2o(tb, 8), t2o(tb, 16)),
                        inc="sv" if tb == 3 else None)
                em.w("s", "sv")
                for tb in TBS:
                    em.s(lambda e, tb=tb: e.activation(
                        t2o(tb, 32), t2o(tb, 24), AF.Sigmoid))
                for tb in TBS:
                    em.s(lambda e, tb=tb: e.activation(
                        t2o(tb, 36), t2o(tb, 24), AF.Sigmoid, scale=-1.0),
                        inc="sa" if tb == 3 else None)
                em.w("v", "sa")
                for tb in TBS:
                    em.v(lambda e, tb=tb: e.tensor_scalar(
                        t2o(tb, 56, EXP), t2o(tb, 48, EXP), t2o(tb, 16), None,
                        ALU.is_equal))
                for tb in TBS:
                    em.v(lambda e, tb=tb: e.tensor_scalar(
                        t2o(tb, 56, EXP), t2o(tb, 56, EXP), t2o(tb, 36), None,
                        ALU.mult))
                for tb in TBS:
                    em.v(lambda e, tb=tb: e.scalar_tensor_tensor(
                        t2o(tb, 48, EXP), t2o(tb, 40, EXP), t2o(tb, 32),
                        t2o(tb, 56, EXP), ALU.mult, ALU.add))
                for tb in TBS:
                    em.v(lambda e, tb=tb: e.tensor_copy(
                        t2o(tb, 60, EXP, bf=True), t2o(tb, 48, EXP)),
                        inc="sv" if tb == 3 else None)
                for tb in TBS:
                    em.w("t", "sv")
                    em.t(lambda e, tb=tb: e.transpose(
                        ptp.ap()[0:EXP, :], t2o(tb, 60, EXP, bf=True),
                        ident[:, :]), inc="sp")
                    em.w("v", "sp")
                    em.v(lambda e, tb=tb: e.tensor_copy(
                        wfm[:, tb * 128:(tb + 1) * 128], ptp.ap()[0:EXP, :]),
                        inc="sv")
                em.w("q1", "sv")
                em.q1(lambda e: e.dma_start(out=rwb[32:40, :], in_=wfm[:, :]))
                he_dma = {}
                for e8 in range(EXP):
                    gw = e8 % 2
                    em.w("q2", "q1")
                    if e8 >= 2:
                        em.wv("q2", "sv", em.c["sv"])
                    em.q2(lambda e, e8=e8, gw=gw: e.dma_start(
                        out=bA(e_wer + gw * CK // 2, CK),
                        in_=rwb[32 + e8:33 + e8, :].partition_broadcast(128)))
                    if e8 >= 1:
                        em.wv("q1", "sp", em.c["sp"])
                    for k in range(NCH):
                        em.q1(lambda e, e8=e8, k=k: e.dma_start(
                            out=bA(e_w1s + k * FF // 2, FF),
                            in_=w1b[e8, k * 128:(k + 1) * 128, :]))
                    em.wv("t", "q1", em.c["q1"])
                    ev_f = {}
                    for f in range(NFC):
                        oi = f % 4
                        if f >= 4:
                            em.wv("t", "sa", ev_f[oi])
                        for k in range(NCH):
                            em.t(lambda e, k=k, f=f, oi=oi: e.matmul(
                                pm[oi].ap()[:, :],
                                bA(e_w1s + k * FF // 2, FF)[
                                    :, f * 128:(f + 1) * 128],
                                bA(e_x1b + k * CK // 2, CK),
                                start=(k == 0), stop=(k == NCH - 1)),
                                inc="sp" if k == NCH - 1 else None)
                        gr = f % 2
                        em.wv("s", "sp", em.c["sp"])
                        if f >= 2:
                            em.wv("s", "sv", em.c["sv"])
                        em.s(lambda e, e8=e8, f=f, oi=oi, gr=gr: e.activation(
                            bA(e_rlb + gr * CK // 2, CK), pm[oi].ap()[:, :],
                            AF.Relu, bias=ccol("b1c", e8 * NFC + f)), inc="sa")
                        ev_f[oi] = em.c["sa"]
                        em.wv("v", "sa", em.c["sa"])
                        em.wv("v", "q2", em.c["q2"])
                        hq = (e8 * NFC + f) % 2
                        if e8 * NFC + f >= 2:
                            em.wv("v", "q1", he_dma[hq])
                        em.v(lambda e, gr=gr, gw=gw, hq=hq: e.tensor_mul(
                            bA(e_hrt + hq * CK // 2, CK),
                            bA(e_rlb + gr * CK // 2, CK),
                            bA(e_wer + gw * CK // 2, CK)), inc="sv")
                        em.wv("q1", "sv", em.c["sv"])
                        em.q1(lambda e, e8=e8, f=f, hq=hq: e.dma_start(
                            out=hed[e8, f * 128:(f + 1) * 128, :],
                            in_=bA(e_hrt + hq * CK // 2, CK)))
                        he_dma[hq] = em.c["q1"]
                em.w("q1", "q1")
                for hf in range(2):
                    em.w("t", "sv", "sa")
                    first = True
                    pe_ms = {}
                    for e8 in range(EXP):
                        for f in range(NFC):
                            idx = e8 * NFC + f
                            gp = idx % 2
                            hq = 2 + idx % 2
                            if idx >= 2:
                                em.wv("q1", "sp", pe_ms[idx - 2])
                            em.q1(lambda e, e8=e8, f=f, hf=hf, gp=gp:
                                  e.dma_start(
                                      out=bA(e_w2s + gp * CK // 2, CK),
                                      in_=w2b[e8, f * 128:(f + 1) * 128,
                                              hf * 512:(hf + 1) * 512]))
                            em.q1(lambda e, e8=e8, f=f, hq=hq: e.dma_start(
                                out=bA(e_hrt + hq * CK // 2, CK),
                                in_=hed[e8, f * 128:(f + 1) * 128, :]))
                            em.wv("t", "q1", em.c["q1"])
                            for oi in range(4):
                                em.t(lambda e, oi=oi, gp=gp, hq=hq,
                                     first=first: e.matmul(
                                         pm[oi].ap()[:, :],
                                         bA(e_w2s + gp * CK // 2, CK)[
                                             :, oi * 128:(oi + 1) * 128],
                                         bA(e_hrt + hq * CK // 2, CK),
                                         start=first, stop=False),
                                     inc="sp" if oi == 3 else None)
                            pe_ms[idx] = em.c["sp"]
                            first = False
                    for oi in range(4):
                        d = hf * 4 + oi
                        em.t(lambda e, oi=oi, d=d: e.matmul(
                            pm[oi].ap()[:, :], b2bf[:, d * 128:(d + 1) * 128],
                            wfm[:, :], start=False, stop=True),
                            inc="sp" if oi == 3 else None)
                    em.w("v", "sp")
                    for oi in range(4):
                        d = hf * 4 + oi
                        em.v(lambda e, d=d, oi=oi: e.tensor_add(
                            fA(e_m2 + d * CK, CK), fA(e_x1 + d * CK, CK),
                            pm[oi].ap()[:, :]),
                            inc="sv" if oi == 3 else None)
                ln_stats(e_m2, e_sq, NCH)
                row_stats_close(e_mur, e_rsr, invH, e_row, e_lnt)
                em.w("v", "q2")
                for k in range(NCH):
                    em.v(lambda e, k=k: e.tensor_sub(
                        fA(e_lnt, CK), fA(e_m2 + k * CK, CK), fA(e_mur, CK)))
                    em.v(lambda e, k=k: e.scalar_tensor_tensor(
                        fA(e_fs, CK), fA(e_lnt, CK), ccol("gam", k),
                        fA(e_rsr, CK), ALU.mult, ALU.mult))
                    em.v(lambda e, k=k: e.tensor_scalar(
                        fA(e_fs, CK), fA(e_fs, CK), ccol("bet", k), None,
                        ALU.add))
                    em.v(lambda e, k=k: e.tensor_add(
                        fA(e_x1 + k * CK, CK), fA(e_x1 + k * CK, CK),
                        fA(e_fs, CK)),
                        inc="sv" if k == NCH - 1 else None)
                ln_stats(e_x1, e_sq, NCH)
                row_stats_close(e_mur, e_rsr, invH, e_row, e_lnt)
                em.w("v", "q2")
                for k in range(NCH):
                    em.v(lambda e, k=k: e.tensor_sub(
                        fA(e_lnt, CK), fA(e_x1 + k * CK, CK), fA(e_mur, CK)))
                    em.v(lambda e, k=k: e.tensor_mul(
                        fA(e_x1 + k * CK, CK), fA(e_lnt, CK), fA(e_rsr, CK)),
                        inc="sv" if k == NCH - 1 else None)
                em.w("q1", "sv")
                for k in range(NCH):
                    em.q1(lambda e, k=k, sl=sl: e.dma_start(
                        out=outt[k * 128:(k + 1) * 128, sl],
                        in_=fA(e_x1 + k * CK, CK)))
            em.w("q1", "q1")
            em.w("q1", "q2")

        with nc.Block() as block:
            @block.sync
            def _(eng):
                sched(Em("q1", eng))

            @block.gpsimd
            def _(eng):
                sched(Em("q2", eng))

            @block.tensor
            def _(eng):
                sched(Em("t", eng))

            @block.vector
            def _(eng):
                sched(Em("v", eng))

            @block.scalar
            def _(eng):
                sched(Em("s", eng))

    return nc


def kernel(**inp):
    hid = np.asarray(inp["L0_attn_hidden_states"], np.float32)
    wiT = np.ascontiguousarray(
        np.asarray(inp["L0_attn_in_proj_w"], np.float32).transpose(0, 2, 1))
    woT = np.ascontiguousarray(
        np.asarray(inp["L0_attn_out_proj_w"], np.float32).transpose(0, 2, 1))
    xp = np.asarray(inp["L0_attn_x_proj_w"], np.float32)
    wxp = np.zeros((L, DI, 128), np.float32)
    wxp[:, :, 0] = xp[:, 0, :]
    wxp[:, :, 32:48] = xp[:, 1:17, :].transpose(0, 2, 1)
    wxp[:, :, 64:80] = xp[:, 17:33, :].transpose(0, 2, 1)
    o1T = np.ascontiguousarray(np.asarray(inp["L0_attn_op1_w"], np.float32).T)
    o2T = np.ascontiguousarray(np.asarray(inp["L0_attn_op2_w"], np.float32).T)
    w1 = np.ascontiguousarray(np.asarray(inp["L0_ffn_W1"], np.float32))
    w2 = np.ascontiguousarray(np.asarray(inp["L0_ffn_W2"], np.float32))
    wg = np.ascontiguousarray(np.asarray(inp["L0_ffn_Wg"], np.float32))
    b2 = np.ascontiguousarray(np.asarray(inp["L0_ffn_b2"], np.float32))

    colp = np.zeros((128, NCOL), np.float32)

    def put(name, v):
        c = _cols(v)
        colp[:, COLL[name]:COLL[name] + c.shape[1]] = c

    put("inng", inp["L0_attn_in_norm_g"])
    put("innb", inp["L0_attn_in_norm_b"])
    put("lng", inp["L0_attn_ln_g"])
    put("lnb", inp["L0_attn_ln_b"])
    put("convw", np.asarray(inp["L0_attn_conv_w"],
                            np.float32).transpose(0, 1, 3, 2))
    put("convb", inp["L0_attn_conv_b"])
    put("dtw", np.asarray(inp["L0_attn_dt_proj_w"], np.float32)[:, :, 0])
    put("dtb", inp["L0_attn_dt_proj_b"])
    put("Dd", inp["L0_attn_D"])
    put("mlg", inp["L0_attn_mamba_ln_g"])
    put("mlb", inp["L0_attn_mamba_ln_b"])
    put("op1b", inp["L0_attn_op1_b"])
    put("op2b", inp["L0_attn_op2_b"])
    put("b1c", inp["L0_ffn_b1"])
    put("gam", inp["L0_ffn_gamma"])
    put("bet", inp["L0_ffn_beta"])

    bgr = np.ascontiguousarray(
        np.repeat(np.asarray(inp["L0_ffn_bg"], np.float32)[None, :], 128, 0))
    idb = np.eye(128, dtype=BF)
    _cv = [EPS, -0.5, -1.0, 1.0] + [-float(n) for n in range(2, 17)] + [0.0] * 5
    cstv = np.ascontiguousarray(np.repeat(np.array([_cv], np.float32), 128, 0))

    nc = build_nc()
    shared = dict(wi=wiT, wo=woT, wx=wxp, o1w=o1T, o2w=o2T, w1i=w1, w2i=w2,
                  wgi=wg, b2r=b2, colp=colp, bgr=bgr, idb=idb, cstv=cstv)
    in_maps = []
    for c in range(NCORE):
        m = dict(shared)
        m["x0t"] = np.ascontiguousarray(hid[c].T)
        in_maps.append(m)
    res = run_bass_kernel_spmd(nc, in_maps, core_ids=list(range(NCORE)))
    out = np.stack([np.asarray(r["outt"]).T for r in res.results], 0)
    return np.ascontiguousarray(out.astype(np.float32))


# version 05
# speedup vs baseline: 1.2274x; 1.2274x over previous
import contextlib
import numpy as np
import ml_dtypes
import concourse.bass as bass
import concourse.mybir as mybir
from concourse.bass_utils import run_bass_kernel_spmd

F32 = mybir.dt.float32
BF16 = mybir.dt.bfloat16
AF = mybir.ActivationFunctionType
ALU = mybir.AluOpType
AX = mybir.AxisListType

H, DI, T, L, NS, KTAP = 1024, 2048, 4096, 4, 16, 4
CK = 512
NT = T // CK
NCH = H // 128
NDC = DI // 128
EXP, FF = 8, 2048
NFC = FF // 128
EPS = 1e-5
NCORE = 8
BF = ml_dtypes.bfloat16
ARENA_COLS = 49800


def _col_layout():
    lay, off = {}, [0]

    def add(name, n):
        lay[name] = off[0]
        off[0] += n

    add("inng", NCH); add("innb", NCH)
    add("lng", L * NCH); add("lnb", L * NCH)
    add("convw", L * 2 * KTAP * NDC); add("convb", L * 2 * NDC)
    add("dtw", L * NDC); add("dtb", L * NDC)
    add("Dd", L * NDC); add("mlg", L * NDC); add("mlb", L * NDC)
    add("op1b", NCH); add("op2b", NCH)
    add("b1c", EXP * NFC)
    add("gam", NCH); add("bet", NCH)
    return lay, off[0]


COLL, NCOL = _col_layout()


def _cols(v):
    v = np.asarray(v, np.float32)
    lead, C = v.shape[:-1], v.shape[-1]
    r = v.reshape(*lead, C // 128, 128)
    r = np.moveaxis(r, -1, 0)
    return np.ascontiguousarray(r.reshape(128, -1))


class Arena:
    def __init__(self, n):
        self.n, self.off = n, 0

    def alloc(self, cols):
        cols = (cols + 7) & ~7
        o = self.off
        self.off += cols
        assert self.off <= self.n, f"arena overflow {self.off} > {self.n}"
        return o


def build_nc():
    nc = bass.Bass("TRN2", target_bir_lowering=False)

    x0t = nc.dram_tensor("x0t", [H, T], F32, kind="ExternalInput").ap()
    wi = nc.dram_tensor("wi", [L, H, 2 * DI], F32, kind="ExternalInput").ap()
    wo = nc.dram_tensor("wo", [L, DI, H], F32, kind="ExternalInput").ap()
    wx = nc.dram_tensor("wx", [L, DI, 128], F32, kind="ExternalInput").ap()
    o1w = nc.dram_tensor("o1w", [H, H], F32, kind="ExternalInput").ap()
    o2w = nc.dram_tensor("o2w", [H, H], F32, kind="ExternalInput").ap()
    w1i = nc.dram_tensor("w1i", [EXP, H, FF], F32, kind="ExternalInput").ap()
    w2i = nc.dram_tensor("w2i", [EXP, FF, H], F32, kind="ExternalInput").ap()
    wgi = nc.dram_tensor("wgi", [H, EXP], F32, kind="ExternalInput").ap()
    b2r = nc.dram_tensor("b2r", [EXP, H], F32, kind="ExternalInput").ap()
    colp = nc.dram_tensor("colp", [128, NCOL], F32, kind="ExternalInput").ap()
    bgr = nc.dram_tensor("bgr", [128, EXP], F32, kind="ExternalInput").ap()
    idb = nc.dram_tensor("idb", [128, 128], BF16, kind="ExternalInput").ap()
    cstv = nc.dram_tensor("cstv", [128, 24], F32, kind="ExternalInput").ap()

    wib = nc.dram_tensor("wib", [L, H, 2 * DI], BF16, kind="Internal").ap()
    wob = nc.dram_tensor("wob", [L, DI, H], BF16, kind="Internal").ap()
    w1b = nc.dram_tensor("w1b", [EXP, H, FF], BF16, kind="Internal").ap()
    w2b = nc.dram_tensor("w2b", [EXP, FF, H], BF16, kind="Internal").ap()
    wmb = nc.dram_tensor("wmb", [2, H, H], BF16, kind="Internal").ap()
    xs = nc.dram_tensor("xs", [H, T], F32, kind="Internal").ap()
    ydr = nc.dram_tensor("ydr", [DI, CK], BF16, kind="Internal").ap()
    sgd = nc.dram_tensor("sgd", [DI, CK], BF16, kind="Internal").ap()
    hed = nc.dram_tensor("hed", [EXP, FF, CK], BF16, kind="Internal").ap()
    rwf = nc.dram_tensor("rwf", [8, CK], F32, kind="Internal").ap()
    rwb = nc.dram_tensor("rwb", [48, CK], BF16, kind="Internal").ap()

    outt = nc.dram_tensor("outt", [H, T], F32, kind="ExternalOutput").ap()

    ctx = contextlib.ExitStack()
    with ctx:
        arena_t = ctx.enter_context(nc.sbuf_tensor("arena", [128, ARENA_COLS], F32))
        colsb_t = ctx.enter_context(nc.sbuf_tensor("colsb", [128, NCOL], F32))
        bg_t = ctx.enter_context(nc.sbuf_tensor("bg", [128, EXP], F32))
        id_t = ctx.enter_context(nc.sbuf_tensor("identb", [128, 128], BF16))
        one_t = ctx.enter_context(nc.sbuf_tensor("onecol", [128, 1], BF16))
        hst_t = ctx.enter_context(nc.sbuf_tensor("hstate", [128, NDC * NS], F32))
        wgb_t = ctx.enter_context(nc.sbuf_tensor("wgb", [128, NCH * EXP], BF16))
        b2b_t = ctx.enter_context(nc.sbuf_tensor("b2b", [EXP, H], BF16))
        wxb_t = ctx.enter_context(nc.sbuf_tensor("wxb", [128, NDC * 128], BF16))
        wfm_t = ctx.enter_context(nc.sbuf_tensor("wfm", [EXP, CK], BF16))
        cst_t = ctx.enter_context(nc.sbuf_tensor("cst", [128, 24], F32))

        pm = [ctx.enter_context(nc.psum_tensor(f"pm{i}", [128, 512], F32))
              for i in range(4)]
        ps1 = ctx.enter_context(nc.psum_tensor("ps1", [1, 512], F32))
        ps2 = ctx.enter_context(nc.psum_tensor("ps2", [1, 512], F32))
        plg = ctx.enter_context(nc.psum_tensor("plg", [128, 32], F32))
        ptp = ctx.enter_context(nc.psum_tensor("ptp", [128, 128], BF16))

        SEM = {s: ctx.enter_context(nc.semaphore(s))
               for s in ("q1", "q2", "sp", "sa", "sv")}

        arena = arena_t.ap()
        arena_bf = arena.bitcast(BF16)
        colsb = colsb_t.ap()
        bg_sb = bg_t.ap()
        ident = id_t.ap()
        onec = one_t.ap()
        hstate = hst_t.ap()
        wgbf = wgb_t.ap()
        b2bf = b2b_t.ap()
        wxbf = wxb_t.ap()
        wfm = wfm_t.ap()
        cst = cst_t.ap()
        _cvals = [EPS, -0.5, -1.0, 1.0] + [-float(n) for n in range(2, 17)]
        for ci_, cv_ in enumerate(_cvals):
            nc.const_aps.register(F32, cv_, cst[:, ci_:ci_ + 1])

        def fA(off, n, p0=0, p1=128):
            return arena[p0:p1, off:off + n]

        def bA(off_f32, nbf, p0=0, p1=128):
            return arena_bf[p0:p1, 2 * off_f32:2 * off_f32 + nbf]

        def ccol(name, idx):
            return colsb[:, COLL[name] + idx:COLL[name] + idx + 1]

        am = Arena(ARENA_COLS)
        o_xch = am.alloc(NCH * CK)
        o_abf = am.alloc(4 * CK)
        o_xsq = am.alloc(NCH * CK // 2)
        o_xnb = o_xsq
        o_R1 = am.alloc(NDC * 520)
        o_xiC = am.alloc(NDC * CK // 2)
        o_sgR = am.alloc(4 * CK // 2)
        o_pjB = am.alloc(CK // 2)
        o_pjC = am.alloc(CK // 2)
        o_Bre = am.alloc(NS * CK // 2)
        o_Cre = am.alloc(NS * CK // 2)
        o_rre = am.alloc(CK)
        o_mur = am.alloc(CK)
        o_rsr = am.alloc(CK)
        o_ab = am.alloc(CK + CK // 2)
        o_dts = am.alloc(2 * 1536)
        o_ydc = am.alloc(2 * CK // 2)
        o_ysq = am.alloc(2 * CK // 2)
        o_dxc = am.alloc(CK // 2)
        o_dia = am.alloc(2 * NDC * KTAP * 64)
        o_wsl = am.alloc(8 * CK // 2)
        o_wos = am.alloc(2 * CK // 2)
        o_tlA = am.alloc(NDC * 4)
        o_tlB = am.alloc(NDC * 4)
        o_row = am.alloc(2 * CK)
        o_lnt = am.alloc(CK)
        M_TOP = am.off

        ap_ = Arena(ARENA_COLS)
        p_xch = ap_.alloc(NCH * CK)
        p_xbf = ap_.alloc(NCH * CK // 2)
        p_g1 = ap_.alloc(NCH * CK // 2)
        p_bq = None
        p_ub = ap_.alloc(NCH * CK)
        p_x0c = ap_.alloc(NCH * CK)
        p_ws = ap_.alloc(8 * CK // 2)
        p_sq = ap_.alloc(NCH * CK // 2)
        p_mur = ap_.alloc(CK)
        p_rsr = ap_.alloc(CK)
        p_row = ap_.alloc(2 * CK)
        p_lnt = ap_.alloc(CK)

        ae = Arena(ARENA_COLS)
        e_x1 = ae.alloc(NCH * CK)
        e_x1b = ae.alloc(NCH * CK // 2)
        e_m2 = ae.alloc(NCH * CK)
        e_w1s = ae.alloc(2 * NCH * FF // 2)
        e_w2s = ae.alloc(4 * CK // 2)
        e_wer = ae.alloc(2 * CK // 2)
        e_rlb = ae.alloc(2 * CK // 2)
        e_hrt = ae.alloc(4 * CK // 2)
        e_t2 = ae.alloc(4 * 64)
        e_sq = ae.alloc(NCH * CK // 2)
        e_mur = ae.alloc(CK)
        e_rsr = ae.alloc(CK)
        e_row = ae.alloc(2 * CK)
        e_lnt = ae.alloc(CK)
        e_fs = ae.alloc(CK)

        ac = Arena(ARENA_COLS)
        c_f = [ac.alloc(2048) for _ in range(2)]
        c_b = [ac.alloc(1024) for _ in range(2)]

        class Em:
            def __init__(self, want, eng):
                self.want, self.eng = want, eng
                self.c = {k: 0 for k in SEM}

            def _go(self, key, fn, inc, n):
                if self.want == key:
                    ins = fn(self.eng)
                    if inc is not None:
                        ins.then_inc(SEM[inc], n)
                if inc is not None:
                    self.c[inc] += n

            def t(self, fn, inc=None):
                self._go("t", fn, inc, 1)

            def v(self, fn, inc=None):
                self._go("v", fn, inc, 1)

            def s(self, fn, inc=None):
                self._go("s", fn, inc, 1)

            def q1(self, fn):
                self._go("q1", fn, "q1", 16)

            def q2(self, fn):
                self._go("q2", fn, "q2", 16)

            def w(self, key, *sems):
                if self.want == key:
                    for s_ in sems:
                        self.eng.wait_ge(SEM[s_], self.c[s_])

            def wv(self, key, sem, val):
                if self.want == key:
                    self.eng.wait_ge(SEM[sem], val)

            def barrier(self):
                for k in ("t", "v", "s", "q1", "q2"):
                    self.w(k, "q1", "q2", "sp", "sa", "sv")

        def sched(em: Em):
            invH = 1.0 / H
            invDI = 1.0 / DI

            em.q1(lambda e: e.dma_start(out=cst[:, :], in_=cstv[:, :]))
            em.q1(lambda e: e.dma_start(out=colsb[:, :], in_=colp[:, :]))
            em.q1(lambda e: e.dma_start(out=bg_sb[:, :], in_=bgr[:, :]))
            em.q1(lambda e: e.dma_start(out=ident[:, :], in_=idb[:, :]))
            em.w("v", "q1")
            em.v(lambda e: e.memset(onec[:, :], 1.0))
            em.v(lambda e: e.memset(hstate[:, :], 0.0), inc="sv")

            for k in range(NCH):
                em.wv("q1", "sv", em.c["sv"])
                em.q1(lambda e, k=k: e.dma_start(
                    out=fA(c_f[0], EXP), in_=wgi[k * 128:(k + 1) * 128, :]))
                em.w("v", "q1")
                em.v(lambda e, k=k: e.tensor_copy(
                    wgbf[:, k * EXP:(k + 1) * EXP], fA(c_f[0], EXP)), inc="sv")
            em.w("q1", "sv")
            em.q1(lambda e: e.dma_start(out=fA(c_f[1], 1024, 0, 8), in_=b2r[:, :]))
            em.w("v", "q1")
            em.v(lambda e: e.tensor_copy(b2bf[:, :], fA(c_f[1], 1024, 0, 8)),
                 inc="sv")

            jobs = []
            for l in range(L):
                for r in range(NCH):
                    for c2 in range(2):
                        jobs.append((wi[l, r * 128:(r + 1) * 128,
                                        c2 * 2048:(c2 + 1) * 2048],
                                     wib[l, r * 128:(r + 1) * 128,
                                         c2 * 2048:(c2 + 1) * 2048], 2048))
                for r in range(NDC):
                    jobs.append((wo[l, r * 128:(r + 1) * 128, :],
                                 wob[l, r * 128:(r + 1) * 128, :], 1024))
            for e8 in range(EXP):
                for r in range(NCH):
                    jobs.append((w1i[e8, r * 128:(r + 1) * 128, :],
                                 w1b[e8, r * 128:(r + 1) * 128, :], 2048))
                for r in range(NFC):
                    jobs.append((w2i[e8, r * 128:(r + 1) * 128, :],
                                 w2b[e8, r * 128:(r + 1) * 128, :], 1024))
            for m, wsrc in ((0, o1w), (1, o2w)):
                for r in range(NCH):
                    jobs.append((wsrc[r * 128:(r + 1) * 128, :],
                                 wmb[m, r * 128:(r + 1) * 128, :], 1024))

            ld_ms, cp_ms, st_ms = [], [], []
            for i, (src, dst, n) in enumerate(jobs):
                p = i % 2
                if i >= 2:
                    em.wv("q1", "sv", cp_ms[i - 2])
                em.q1(lambda e, src=src, p=p, n=n: e.dma_start(
                    out=fA(c_f[p], n), in_=src))
                ld_ms.append(em.c["q1"])
                if i >= 2:
                    em.wv("v", "q1", st_ms[i - 2])
                em.wv("v", "q1", ld_ms[i])
                em.v(lambda e, p=p, n=n: e.tensor_copy(
                    bA(c_b[p], n), fA(c_f[p], n)), inc="sv")
                cp_ms.append(em.c["sv"])
                em.wv("q1", "sv", cp_ms[i])
                em.q1(lambda e, dst=dst, p=p, n=n: e.dma_start(
                    out=dst, in_=bA(c_b[p], n)))
                st_ms.append(em.c["q1"])
            em.barrier()

            def row_stats_close(o_murep, o_rsrep, invN, ro, ro2):
                r0 = fA(ro + 0, CK, 0, 1)
                r1 = fA(ro + CK, CK, 0, 1)
                r2 = fA(ro2, CK, 0, 1)
                em.w("v", "sp", "q1")
                em.v(lambda e: e.tensor_scalar(r0, ps1.ap()[0:1, :], invN, None,
                                               ALU.mult))
                em.v(lambda e: e.tensor_scalar(r1, ps2.ap()[0:1, :], invN, None,
                                               ALU.mult))
                em.v(lambda e: e.tensor_mul(r2, r0, r0))
                em.v(lambda e: e.tensor_sub(r1, r1, r2), inc="sv")
                em.w("s", "sv")
                em.s(lambda e: e.activation(r2, r1, AF.Ln, bias=EPS))
                em.s(lambda e: e.activation(r1, r2, AF.Exp, scale=-0.5), inc="sa")
                em.w("q1", "sa")
                em.q1(lambda e: e.dma_start(out=rwf[1:2, :], in_=r0))
                em.q1(lambda e: e.dma_start(out=rwf[2:3, :], in_=r1))
                em.w("q2", "q1")
                em.q2(lambda e: e.dma_start(
                    out=fA(o_murep, CK),
                    in_=rwf[1:2, :].partition_broadcast(128)))
                em.q2(lambda e: e.dma_start(
                    out=fA(o_rsrep, CK),
                    in_=rwf[2:3, :].partition_broadcast(128)))

            def ln_stats(o_src, o_sq, nch):
                em.w("s", "q1", "sv")
                for k in range(nch):
                    em.s(lambda e, k=k: e.activation(
                        bA(o_sq + k * CK // 2, CK), fA(o_src + k * CK, CK),
                        AF.Square), inc="sa" if k == nch - 1 else None)
                em.w("t", "sv", "sa", "q1")
                for k in range(nch):
                    em.t(lambda e, k=k: e.matmul(
                        ps1.ap()[0:1, :], cst[:, 3:4], fA(o_src + k * CK, CK),
                        start=(k == 0), stop=(k == nch - 1)))
                    em.t(lambda e, k=k: e.matmul(
                        ps2.ap()[0:1, :], onec[:, :], bA(o_sq + k * CK // 2, CK),
                        start=(k == 0), stop=(k == nch - 1)),
                        inc="sp" if k == nch - 1 else None)

            for j in range(NT):
                sl = slice(j * CK, (j + 1) * CK)
                em.barrier()
                for k in range(NCH):
                    em.q1(lambda e, k=k, sl=sl: e.dma_start(
                        out=fA(o_xch + k * CK, CK),
                        in_=x0t[k * 128:(k + 1) * 128, sl]))
                ln_stats(o_xch, o_xsq, NCH)
                row_stats_close(o_mur, o_rsr, invH, o_row, o_lnt)
                em.w("v", "q2")
                for k in range(NCH):
                    em.v(lambda e, k=k: e.tensor_sub(
                        fA(o_lnt, CK), fA(o_xch + k * CK, CK), fA(o_mur, CK)))
                    em.v(lambda e, k=k: e.scalar_tensor_tensor(
                        fA(o_xch + k * CK, CK), fA(o_lnt, CK), ccol("inng", k),
                        fA(o_rsr, CK), ALU.mult, ALU.mult))
                    em.v(lambda e, k=k: e.tensor_scalar(
                        fA(o_xch + k * CK, CK), fA(o_xch + k * CK, CK),
                        ccol("innb", k), None, ALU.add),
                        inc="sv" if k == NCH - 1 else None)
                em.w("q1", "sv")
                for k in range(NCH):
                    em.q1(lambda e, k=k, sl=sl: e.dma_start(
                        out=xs[k * 128:(k + 1) * 128, sl],
                        in_=fA(o_xch + k * CK, CK)))
            em.barrier()

            for l in range(L):
                em.v(lambda e: e.memset(hstate[:, :], 0.0))
                for dc in range(NDC):
                    em.wv("q1", "sv", em.c["sv"])
                    em.q1(lambda e, l=l, dc=dc: e.dma_start(
                        out=fA(o_ab, 128), in_=wx[l, dc * 128:(dc + 1) * 128, :]))
                    em.w("v", "q1")
                    em.v(lambda e, dc=dc: e.tensor_copy(
                        wxbf[:, dc * 128:(dc + 1) * 128], fA(o_ab, 128)),
                        inc="sv")
                for c2 in range(2):
                    for dc in range(NDC):
                        for k in range(KTAP):
                            di = ((c2 * NDC + dc) * KTAP + k) * 64
                            ci = ((l * 2 + c2) * KTAP + k) * NDC + dc
                            em.v(lambda e, di=di, ci=ci: e.tensor_scalar(
                                bA(o_dia + di, 128), ident[:, :],
                                ccol("convw", ci), None, ALU.mult))
                em.v(lambda e: e.memset(bA(o_tlA, 4 * NDC), 0.0))
                em.v(lambda e: e.memset(bA(o_tlB, 4 * NDC), 0.0), inc="sv")
                em.barrier()

                for j in range(NT):
                    sl = slice(j * CK, (j + 1) * CK)
                    em.barrier()
                    for k in range(NCH):
                        em.q1(lambda e, k=k, sl=sl: e.dma_start(
                            out=fA(o_xch + k * CK, CK),
                            in_=xs[k * 128:(k + 1) * 128, sl]))
                    ln_stats(o_xch, o_xsq, NCH)
                    row_stats_close(o_mur, o_rsr, invH, o_row, o_lnt)
                    em.w("v", "q2", "sp")
                    for k in range(NCH):
                        em.v(lambda e, k=k: e.tensor_sub(
                            fA(o_lnt, CK), fA(o_xch + k * CK, CK), fA(o_mur, CK)))
                        em.v(lambda e, k=k, l=l: e.scalar_tensor_tensor(
                            bA(o_xnb + k * CK // 2, CK), fA(o_lnt, CK),
                            ccol("lng", l * NCH + k), fA(o_rsr, CK),
                            ALU.mult, ALU.mult))
                        em.v(lambda e, k=k, l=l: e.tensor_scalar(
                            bA(o_xnb + k * CK // 2, CK),
                            bA(o_xnb + k * CK // 2, CK),
                            ccol("lnb", l * NCH + k), None, ALU.add),
                            inc="sv" if k == NCH - 1 else None)
                    for dc in range(NDC):
                        em.v(lambda e, dc=dc: e.tensor_copy(
                            bA(o_R1 + dc * 260, 520)[:, 0:3],
                            bA(o_tlA + dc * 2, 3)))
                        em.v(lambda e, dc=dc: e.tensor_copy(
                            bA(o_R1 + (NDC + dc) * 260, 520)[:, 0:3],
                            bA(o_tlB + dc * 2, 3)),
                            inc="sv" if dc == NDC - 1 else None)

                    evac_ms = {}
                    sg_dma = {}
                    mm_ms = {}
                    for og in range(8):
                        if og >= 1:
                            em.wv("q1", "sp", mm_ms[og - 1])
                        for k in range(NCH):
                            em.q1(lambda e, l=l, k=k, og=og: e.dma_start(
                                out=bA(o_wsl + k * CK // 2, CK),
                                in_=wib[l, k * 128:(k + 1) * 128,
                                        og * 512:(og + 1) * 512]))
                        ld = em.c["q1"]
                        em.wv("t", "q1", ld)
                        em.w("t", "sv")
                        for oi in range(4):
                            o = og * 4 + oi
                            if o >= 4:
                                em.wv("t", "sa", evac_ms.get(oi, 0))
                            for k in range(NCH):
                                em.t(lambda e, k=k, oi=oi: e.matmul(
                                    pm[oi].ap()[:, :],
                                    bA(o_wsl + k * CK // 2, CK)[
                                        :, oi * 128:(oi + 1) * 128],
                                    bA(o_xnb + k * CK // 2, CK),
                                    start=(k == 0), stop=(k == NCH - 1)),
                                    inc="sp" if k == NCH - 1 else None)
                            if oi == 3:
                                mm_ms[og] = em.c["sp"]
                            em.wv("s", "sp", em.c["sp"])
                            if o < NDC:
                                em.s(lambda e, o=o, oi=oi: e.activation(
                                    bA(o_R1 + o * 260, 520)[:, 3:3 + CK],
                                    pm[oi].ap()[:, :], AF.Silu), inc="sa")
                            else:
                                gp2 = (o - NDC) % 2
                                if o - NDC >= 2:
                                    em.wv("s", "q1", sg_dma[gp2])
                                em.s(lambda e, gp2=gp2, oi=oi: e.activation(
                                    bA(o_sgR + gp2 * CK // 2, CK),
                                    pm[oi].ap()[:, :], AF.Sigmoid), inc="sa")
                                em.wv("q1", "sa", em.c["sa"])
                                em.q1(lambda e, o=o, gp2=gp2: e.dma_start(
                                    out=sgd[(o - NDC) * 128:(o - NDC + 1) * 128, :],
                                    in_=bA(o_sgR + gp2 * CK // 2, CK)))
                                sg_dma[gp2] = em.c["q1"]
                            evac_ms[oi] = em.c["sa"]

                    for c2 in range(2):
                        em.w("t", "sa", "sv")
                        cv_ms = {}
                        for dc in range(NDC):
                            oi = dc % 2
                            if dc >= 2:
                                em.wv("t", "sa", cv_ms[oi])
                            src = o_R1 + (c2 * NDC + dc) * 260
                            for k in range(KTAP):
                                di = ((c2 * NDC + dc) * KTAP + k) * 64
                                em.t(lambda e, di=di, src=src, k=k, oi=oi:
                                     e.matmul(pm[oi].ap()[:, :],
                                              bA(o_dia + di, 128),
                                              bA(src, 520)[:, k:k + CK],
                                              start=(k == 0),
                                              stop=(k == KTAP - 1)),
                                     inc="sp" if k == KTAP - 1 else None)
                            em.wv("s", "sp", em.c["sp"])
                            ci = (l * 2 + c2) * NDC + dc
                            if c2 == 0:
                                dst = o_R1 + (NDC + dc) * 260
                                em.s(lambda e, dst=dst, oi=oi, ci=ci:
                                     e.activation(bA(dst, 520)[:, 3:3 + CK],
                                                  pm[oi].ap()[:, :], AF.Silu,
                                                  bias=ccol("convb", ci)),
                                     inc="sa")
                            else:
                                em.s(lambda e, dc=dc, oi=oi, ci=ci:
                                     e.activation(bA(o_xiC + dc * CK // 2, CK),
                                                  pm[oi].ap()[:, :], AF.Silu,
                                                  bias=ccol("convb", ci)),
                                     inc="sa")
                            cv_ms[oi] = em.c["sa"]
                        em.w("v", "sp", "sa")
                        tl = o_tlA if c2 == 0 else o_tlB
                        base = 0 if c2 == 0 else NDC
                        for dc in range(NDC):
                            em.v(lambda e, tl=tl, dc=dc, base=base: e.tensor_copy(
                                bA(tl + dc * 2, 3),
                                bA(o_R1 + (base + dc) * 260, 520)[:, CK:CK + 3]),
                                inc="sv" if dc == NDC - 1 else None)

                    em.w("t", "sa", "sv")
                    for dc in range(NDC):
                        em.t(lambda e, dc=dc: e.matmul(
                            pm[2].ap()[:, :], wxbf[:, dc * 128:(dc + 1) * 128],
                            bA(o_xiC + dc * CK // 2, CK),
                            start=(dc == 0), stop=(dc == NDC - 1)),
                            inc="sp" if dc == NDC - 1 else None)
                    em.w("v", "sp", "q1")
                    em.v(lambda e: e.tensor_copy(fA(o_row, CK, 0, 1),
                                                 pm[2].ap()[0:1, :]))
                    em.v(lambda e: e.tensor_copy(bA(o_pjB, CK, 0, 16),
                                                 pm[2].ap()[32:48, :]))
                    em.v(lambda e: e.tensor_copy(bA(o_pjC, CK, 0, 16),
                                                 pm[2].ap()[64:80, :]), inc="sv")
                    em.w("q1", "sv")
                    em.q1(lambda e: e.dma_start(out=rwf[0:1, :],
                                                in_=fA(o_row, CK, 0, 1)))
                    em.q1(lambda e: e.dma_start(out=rwb[0:16, :],
                                                in_=bA(o_pjB, CK, 0, 16)))
                    em.q1(lambda e: e.dma_start(out=rwb[16:32, :],
                                                in_=bA(o_pjC, CK, 0, 16)))
                    em.w("q2", "q1")
                    em.q2(lambda e: e.dma_start(
                        out=fA(o_rre, CK),
                        in_=rwf[0:1, :].partition_broadcast(128)))
                    for n in range(NS):
                        em.q2(lambda e, n=n: e.dma_start(
                            out=bA(o_Bre + n * CK // 2, CK),
                            in_=rwb[n:n + 1, :].partition_broadcast(128)))
                        em.q2(lambda e, n=n: e.dma_start(
                            out=bA(o_Cre + n * CK // 2, CK),
                            in_=rwb[16 + n:17 + n, :].partition_broadcast(128)))

                    em.w("s", "q2")
                    em.w("v", "q2")
                    vms, sams, stat_ms, park_ms = {}, {}, {}, {}
                    slot_ms = {}
                    for dc in range(NDC):
                        li = l * NDC + dc
                        p = dc % 2
                        dt_p = o_dts + p * 1536
                        d1_p = dt_p + 512
                        d2_p = dt_p + 1024
                        if dc >= 2:
                            em.wv("s", "sv", vms[dc - 2])
                        em.s(lambda e, li=li, dt_p=dt_p: e.activation(
                            fA(dt_p, CK), fA(o_rre, CK), AF.Identity,
                            bias=ccol("dtb", li), scale=ccol("dtw", li)))
                        em.s(lambda e, dt_p=dt_p, d1_p=d1_p: e.activation(
                            fA(d1_p, CK), fA(dt_p, CK), AF.Abs))
                        em.s(lambda e, d1_p=d1_p, d2_p=d2_p: e.activation(
                            fA(d2_p, CK), fA(d1_p, CK), AF.Exp, scale=-1.0))
                        em.s(lambda e, d1_p=d1_p, d2_p=d2_p: e.activation(
                            fA(d1_p, CK), fA(d2_p, CK), AF.Ln, bias=1.0))
                        em.s(lambda e: e.activation(
                            cst[:, 20:21], cst[:, 20:21], AF.Identity),
                            inc="sa")
                        sams[dc] = em.c["sa"]
                        em.wv("v", "sa", sams[dc])
                        if dc >= 2:
                            em.wv("v", "sp", stat_ms[dc - 2])
                            em.wv("v", "q1", park_ms[dc - 2])
                        em.v(lambda e, dt_p=dt_p, d2_p=d2_p: e.tensor_scalar(
                            fA(d2_p, CK), fA(dt_p, CK), 0.0, None, ALU.max))
                        em.v(lambda e, dt_p=dt_p, d1_p=d1_p, d2_p=d2_p:
                             e.tensor_add(fA(dt_p, CK), fA(d2_p, CK),
                                          fA(d1_p, CK)))
                        em.v(lambda e, dc=dc, dt_p=dt_p: e.tensor_mul(
                            bA(o_dxc, CK), fA(dt_p, CK),
                            bA(o_xiC + dc * CK // 2, CK)), inc="sv")
                        dt_ready = em.c["sv"]
                        em.wv("s", "sv", dt_ready)
                        for g in range(NS // 2):
                            for i in range(2):
                                n = 2 * g + i
                                em.wv("s", "sv", slot_ms.get(n % 4, 0))
                                em.s(lambda e, n=n, dt_p=dt_p: e.activation(
                                    fA(o_abf + (n % 4) * CK, CK), fA(dt_p, CK),
                                    AF.Identity, bias=1.0,
                                    scale=-float(n + 1)),
                                    inc="sa" if i == 1 else None)
                            ag = em.c["sa"]
                            em.wv("v", "sa", ag)
                            for i in range(2):
                                n = 2 * g + i
                                em.v(lambda e, n=n: e.tensor_mul(
                                    bA(o_ab + CK, CK), bA(o_dxc, CK),
                                    bA(o_Bre + n * CK // 2, CK)))
                                em.v(lambda e, n=n, dc=dc:
                                     e.tensor_tensor_scan(
                                         bA(o_R1 + n * 260, CK),
                                         fA(o_abf + (n % 4) * CK, CK),
                                         bA(o_ab + CK, CK),
                                         hstate[:, dc * NS + n:dc * NS + n + 1],
                                         ALU.mult, ALU.add),
                                     inc="sv" if i == 1 else None)
                            slot_ms[(2 * g) % 4] = em.c["sv"]
                            slot_ms[(2 * g + 1) % 4] = em.c["sv"]
                        em.v(lambda e, dt_p=dt_p: e.tensor_copy(
                            fA(o_lnt, 64), fA(dt_p, 64)))
                        em.v(lambda e, dc=dc: e.tensor_copy(
                            hstate[:, dc * NS:(dc + 1) * NS],
                            bA(o_R1, NS * 520).rearrange(
                                "p (n t) -> p n t", t=520)[:, :, CK - 1]))
                        for n in range(NS):
                            em.v(lambda e, n=n: e.tensor_mul(
                                bA(o_R1 + n * 260, CK), bA(o_R1 + n * 260, CK),
                                bA(o_Cre + n * CK // 2, CK)))
                        step = NS
                        while step > 2:
                            half = step // 2
                            for i in range(half):
                                em.v(lambda e, i=i, half=half: e.tensor_add(
                                    bA(o_R1 + i * 260, CK),
                                    bA(o_R1 + i * 260, CK),
                                    bA(o_R1 + (i + half) * 260, CK)))
                            step = half
                        yb = o_ydc + p * CK // 2
                        em.v(lambda e, yb=yb: e.tensor_add(
                            bA(yb, CK), bA(o_R1, CK), bA(o_R1 + 260, CK)))
                        em.v(lambda e, dc=dc, li=li, yb=yb:
                             e.scalar_tensor_tensor(
                                 bA(yb, CK), bA(o_xiC + dc * CK // 2, CK),
                                 ccol("Dd", li), bA(yb, CK),
                                 ALU.mult, ALU.add))
                        em.v(lambda e, yb=yb, p=p: e.tensor_mul(
                            bA(o_ysq + p * CK // 2, CK), bA(yb, CK),
                            bA(yb, CK)), inc="sv")
                        vms[dc] = em.c["sv"]
                        em.wv("t", "sv", vms[dc])
                        em.t(lambda e, dc=dc, yb=yb: e.matmul(
                            ps1.ap()[0:1, :], onec[:, :], bA(yb, CK),
                            start=(dc == 0), stop=(dc == NDC - 1)))
                        em.t(lambda e, dc=dc, p=p: e.matmul(
                            ps2.ap()[0:1, :], onec[:, :],
                            bA(o_ysq + p * CK // 2, CK),
                            start=(dc == 0), stop=(dc == NDC - 1)), inc="sp")
                        stat_ms[dc] = em.c["sp"]
                        em.wv("q1", "sv", vms[dc])
                        em.q1(lambda e, dc=dc, yb=yb: e.dma_start(
                            out=ydr[dc * 128:(dc + 1) * 128, :],
                            in_=bA(yb, CK)))
                        park_ms[dc] = em.c["q1"]
                    row_stats_close(o_mur, o_rsr, invDI, o_row, o_lnt)

                    em.w("q1", "q1")
                    em.w("v", "q2")
                    app_ms = {}
                    for dc in range(NDC):
                        li = l * NDC + dc
                        p = dc % 2
                        yb = o_ydc + p * CK // 2
                        tb_ = o_dts + p * 256
                        if dc >= 2:
                            em.wv("q1", "sv", app_ms[dc - 2])
                        em.q1(lambda e, dc=dc, yb=yb: e.dma_start(
                            out=bA(yb, CK),
                            in_=ydr[dc * 128:(dc + 1) * 128, :]))
                        em.q1(lambda e, dc=dc, p=p: e.dma_start(
                            out=bA(o_sgR + (2 + p) * CK // 2, CK),
                            in_=sgd[dc * 128:(dc + 1) * 128, :]))
                        em.w("v", "q1")
                        em.v(lambda e, yb=yb: e.tensor_sub(
                            fA(o_lnt, CK), bA(yb, CK), fA(o_mur, CK)))
                        em.v(lambda e, li=li, tb_=tb_: e.scalar_tensor_tensor(
                            bA(tb_, CK), fA(o_lnt, CK), ccol("mlg", li),
                            fA(o_rsr, CK), ALU.mult, ALU.mult))
                        em.v(lambda e, li=li, tb_=tb_: e.tensor_scalar(
                            bA(tb_, CK), bA(tb_, CK), ccol("mlb", li),
                            None, ALU.add))
                        em.v(lambda e, dc=dc, p=p, tb_=tb_: e.tensor_mul(
                            bA(o_xiC + dc * CK // 2, CK), bA(tb_, CK),
                            bA(o_sgR + (2 + p) * CK // 2, CK)), inc="sv")
                        app_ms[dc] = em.c["sv"]
                    for hf in range(2):
                        em.w("t", "sv", "sa")
                        pe_ms = {}
                        for dc in range(NDC):
                            gp = dc % 2
                            if dc >= 2:
                                em.wv("q1", "sp", pe_ms[dc - 2])
                            em.q1(lambda e, l=l, dc=dc, hf=hf, gp=gp:
                                  e.dma_start(
                                      out=bA(o_wos + gp * CK // 2, CK),
                                      in_=wob[l, dc * 128:(dc + 1) * 128,
                                              hf * 512:(hf + 1) * 512]))
                            em.wv("t", "q1", em.c["q1"])
                            for oi in range(4):
                                em.t(lambda e, oi=oi, gp=gp, dc=dc: e.matmul(
                                    pm[oi].ap()[:, :],
                                    bA(o_wos + gp * CK // 2, CK)[
                                        :, oi * 128:(oi + 1) * 128],
                                    bA(o_xiC + dc * CK // 2, CK),
                                    start=(dc == 0), stop=(dc == NDC - 1)),
                                    inc="sp" if oi == 3 else None)
                            pe_ms[dc] = em.c["sp"]
                        em.w("v", "sp")
                        for oi in range(4):
                            o = hf * 4 + oi
                            em.v(lambda e, o=o, oi=oi: e.tensor_add(
                                fA(o_xch + o * CK, CK), fA(o_xch + o * CK, CK),
                                pm[oi].ap()[:, :]),
                                inc="sv" if oi == 3 else None)
                        em.w("q1", "sv")
                        for oi in range(4):
                            o = hf * 4 + oi
                            em.q1(lambda e, o=o, sl=sl: e.dma_start(
                                out=xs[o * 128:(o + 1) * 128, sl],
                                in_=fA(o_xch + o * CK, CK)))
                em.barrier()

            for j in range(NT):
                sl = slice(j * CK, (j + 1) * CK)
                em.barrier()
                for k in range(NCH):
                    em.q1(lambda e, k=k, sl=sl: e.dma_start(
                        out=fA(p_xch + k * CK, CK),
                        in_=xs[k * 128:(k + 1) * 128, sl]))
                    em.q1(lambda e, k=k, sl=sl: e.dma_start(
                        out=fA(p_x0c + k * CK, CK),
                        in_=x0t[k * 128:(k + 1) * 128, sl]))
                em.w("v", "q1")
                for k in range(NCH):
                    em.v(lambda e, k=k: e.tensor_copy(
                        bA(p_xbf + k * CK // 2, CK), fA(p_xch + k * CK, CK)),
                        inc="sv" if k == NCH - 1 else None)
                for mat in range(2):
                    ev_ms = {}
                    mm_ms = {}
                    for og in range(2):
                        if og >= 1 or mat == 1:
                            em.wv("q1", "sp", mm_ms.get(og - 1, em.c["sp"]))
                        for k in range(NCH):
                            em.q1(lambda e, k=k, og=og, mat=mat: e.dma_start(
                                out=bA(p_ws + k * CK // 2, CK),
                                in_=wmb[mat, k * 128:(k + 1) * 128,
                                        og * 512:(og + 1) * 512]))
                        em.wv("t", "q1", em.c["q1"])
                        em.w("t", "sv" if mat == 0 else "sa")
                        for oi in range(4):
                            o = og * 4 + oi
                            if o >= 4:
                                em.wv("t", "sa" if mat == 0 else "sv",
                                      ev_ms.get(oi, 0))
                            rhs_off = p_xbf if mat == 0 else p_g1
                            for k in range(NCH):
                                em.t(lambda e, k=k, oi=oi, rhs_off=rhs_off:
                                     e.matmul(
                                         pm[oi].ap()[:, :],
                                         bA(p_ws + k * CK // 2, CK)[
                                             :, oi * 128:(oi + 1) * 128],
                                         bA(rhs_off + k * CK // 2, CK),
                                         start=(k == 0), stop=(k == NCH - 1)),
                                     inc="sp" if k == NCH - 1 else None)
                            if oi == 3:
                                mm_ms[og] = em.c["sp"]
                            if mat == 0:
                                em.wv("s", "sp", em.c["sp"])
                                em.s(lambda e, o=o, oi=oi: e.activation(
                                    bA(p_g1 + o * CK // 2, CK),
                                    pm[oi].ap()[:, :], AF.Gelu,
                                    bias=ccol("op1b", o)), inc="sa")
                                ev_ms[oi] = em.c["sa"]
                            else:
                                em.wv("v", "sp", em.c["sp"])
                                em.v(lambda e, o=o, oi=oi:
                                     e.scalar_tensor_tensor(
                                         fA(p_ub + o * CK, CK),
                                         fA(p_x0c + o * CK, CK), 2.0,
                                         pm[oi].ap()[:, :], ALU.mult, ALU.add))
                                em.v(lambda e, o=o: e.tensor_scalar(
                                    fA(p_ub + o * CK, CK),
                                    fA(p_ub + o * CK, CK),
                                    ccol("op2b", o), None, ALU.add), inc="sv")
                                ev_ms[oi] = em.c["sv"]
                ln_stats(p_ub, p_sq, NCH)
                row_stats_close(p_mur, p_rsr, invH, p_row, p_lnt)
                em.w("v", "q2")
                for k in range(NCH):
                    em.v(lambda e, k=k: e.tensor_sub(
                        fA(p_lnt, CK), fA(p_ub + k * CK, CK), fA(p_mur, CK)))
                    em.v(lambda e, k=k: e.tensor_mul(
                        fA(p_ub + k * CK, CK), fA(p_lnt, CK), fA(p_rsr, CK)),
                        inc="sv" if k == NCH - 1 else None)
                em.w("q1", "sv")
                for k in range(NCH):
                    em.q1(lambda e, k=k, sl=sl: e.dma_start(
                        out=xs[k * 128:(k + 1) * 128, sl],
                        in_=fA(p_ub + k * CK, CK)))
            em.barrier()

            for j in range(NT):
                sl = slice(j * CK, (j + 1) * CK)
                em.barrier()
                for k in range(NCH):
                    em.q1(lambda e, k=k, sl=sl: e.dma_start(
                        out=fA(e_x1 + k * CK, CK),
                        in_=xs[k * 128:(k + 1) * 128, sl]))
                em.w("v", "q1")
                for k in range(NCH):
                    em.v(lambda e, k=k: e.tensor_copy(
                        bA(e_x1b + k * CK // 2, CK), fA(e_x1 + k * CK, CK)),
                        inc="sv" if k == NCH - 1 else None)
                TBS = range(4)

                def t2o(tb, off, n=1, bf=False):
                    base = e_t2 + tb * 64 + off
                    return bA(base, n) if bf else fA(base, n)

                em.w("t", "sv", "sa")
                for tb in TBS:
                    for k in range(NCH):
                        em.t(lambda e, k=k, tb=tb: e.matmul(
                            plg.ap()[:, tb * 8:(tb + 1) * 8],
                            bA(e_x1b + k * CK // 2, CK)[
                                :, tb * 128:(tb + 1) * 128],
                            wgbf[:, k * EXP:(k + 1) * EXP],
                            start=(k == 0), stop=(k == NCH - 1)),
                            inc="sp" if k == NCH - 1 else None)
                em.w("v", "sp")
                for tb in TBS:
                    em.v(lambda e, tb=tb: e.tensor_add(
                        t2o(tb, 0, EXP), bg_sb[:, :],
                        plg.ap()[:, tb * 8:(tb + 1) * 8]))
                for tb in TBS:
                    em.v(lambda e, tb=tb: e.reduce_max(
                        t2o(tb, 8), t2o(tb, 0, EXP), axis=AX.X))
                for tb in TBS:
                    em.v(lambda e, tb=tb: e.tensor_scalar(
                        t2o(tb, 40, EXP), t2o(tb, 0, EXP), t2o(tb, 8), None,
                        ALU.is_equal))
                for tb in TBS:
                    em.v(lambda e, tb=tb: e.scalar_tensor_tensor(
                        t2o(tb, 48, EXP), t2o(tb, 40, EXP), -1e30,
                        t2o(tb, 0, EXP), ALU.mult, ALU.add))
                for tb in TBS:
                    em.v(lambda e, tb=tb: e.reduce_max(
                        t2o(tb, 16), t2o(tb, 48, EXP), axis=AX.X))
                for tb in TBS:
                    em.v(lambda e, tb=tb: e.tensor_sub(
                        t2o(tb, 24), t2o(tb, 8), t2o(tb, 16)),
                        inc="sv" if tb == 3 else None)
                em.w("s", "sv")
                for tb in TBS:
                    em.s(lambda e, tb=tb: e.activation(
                        t2o(tb, 32), t2o(tb, 24), AF.Sigmoid))
                for tb in TBS:
                    em.s(lambda e, tb=tb: e.activation(
                        t2o(tb, 36), t2o(tb, 24), AF.Sigmoid, scale=-1.0),
                        inc="sa" if tb == 3 else None)
                em.w("v", "sa")
                for tb in TBS:
                    em.v(lambda e, tb=tb: e.tensor_scalar(
                        t2o(tb, 56, EXP), t2o(tb, 48, EXP), t2o(tb, 16), None,
                        ALU.is_equal))
                for tb in TBS:
                    em.v(lambda e, tb=tb: e.tensor_scalar(
                        t2o(tb, 56, EXP), t2o(tb, 56, EXP), t2o(tb, 36), None,
                        ALU.mult))
                for tb in TBS:
                    em.v(lambda e, tb=tb: e.scalar_tensor_tensor(
                        t2o(tb, 48, EXP), t2o(tb, 40, EXP), t2o(tb, 32),
                        t2o(tb, 56, EXP), ALU.mult, ALU.add))
                for tb in TBS:
                    em.v(lambda e, tb=tb: e.tensor_copy(
                        t2o(tb, 60, EXP, bf=True), t2o(tb, 48, EXP)),
                        inc="sv" if tb == 3 else None)
                for tb in TBS:
                    em.w("t", "sv")
                    em.t(lambda e, tb=tb: e.transpose(
                        ptp.ap()[0:EXP, :], t2o(tb, 60, EXP, bf=True),
                        ident[:, :]), inc="sp")
                    em.w("v", "sp")
                    em.v(lambda e, tb=tb: e.tensor_copy(
                        wfm[:, tb * 128:(tb + 1) * 128], ptp.ap()[0:EXP, :]),
                        inc="sv")
                em.w("q1", "sv")
                em.q1(lambda e: e.dma_start(out=rwb[32:40, :], in_=wfm[:, :]))
                he_dma = {}
                w1ms = {}
                for e8 in range(EXP):
                    gw = e8 % 2
                    pw = e8 % 2
                    em.w("q2", "q1")
                    if e8 >= 2:
                        em.wv("q2", "sv", em.c["sv"])
                        em.wv("q1", "sp", w1ms[e8 - 2])
                        em.wv("q2", "sp", w1ms[e8 - 2])
                    em.q2(lambda e, e8=e8, gw=gw: e.dma_start(
                        out=bA(e_wer + gw * CK // 2, CK),
                        in_=rwb[32 + e8:33 + e8, :].partition_broadcast(128)))
                    for k in range(NCH):
                        fn = (lambda e, e8=e8, k=k, pw=pw: e.dma_start(
                            out=bA(e_w1s + (pw * NCH + k) * FF // 2, FF),
                            in_=w1b[e8, k * 128:(k + 1) * 128, :]))
                        if k % 2 == 0:
                            em.q1(fn)
                        else:
                            em.q2(fn)
                    em.wv("t", "q1", em.c["q1"])
                    em.wv("t", "q2", em.c["q2"])
                    ev_f = {}
                    for f in range(NFC):
                        oi = f % 4
                        if f >= 4:
                            em.wv("t", "sa", ev_f[oi])
                        for k in range(NCH):
                            em.t(lambda e, k=k, f=f, oi=oi, pw=pw: e.matmul(
                                pm[oi].ap()[:, :],
                                bA(e_w1s + (pw * NCH + k) * FF // 2, FF)[
                                    :, f * 128:(f + 1) * 128],
                                bA(e_x1b + k * CK // 2, CK),
                                start=(k == 0), stop=(k == NCH - 1)),
                                inc="sp" if k == NCH - 1 else None)
                        gr = f % 2
                        em.wv("s", "sp", em.c["sp"])
                        if f >= 2:
                            em.wv("s", "sv", em.c["sv"])
                        em.s(lambda e, e8=e8, f=f, oi=oi, gr=gr: e.activation(
                            bA(e_rlb + gr * CK // 2, CK), pm[oi].ap()[:, :],
                            AF.Relu, bias=ccol("b1c", e8 * NFC + f)), inc="sa")
                        ev_f[oi] = em.c["sa"]
                        em.wv("v", "sa", em.c["sa"])
                        em.wv("v", "q2", em.c["q2"])
                        hq = (e8 * NFC + f) % 2
                        if e8 * NFC + f >= 2:
                            em.wv("v", "q1", he_dma[hq])
                        em.v(lambda e, gr=gr, gw=gw, hq=hq: e.tensor_mul(
                            bA(e_hrt + hq * CK // 2, CK),
                            bA(e_rlb + gr * CK // 2, CK),
                            bA(e_wer + gw * CK // 2, CK)), inc="sv")
                        em.wv("q1", "sv", em.c["sv"])
                        em.q1(lambda e, e8=e8, f=f, hq=hq: e.dma_start(
                            out=hed[e8, f * 128:(f + 1) * 128, :],
                            in_=bA(e_hrt + hq * CK // 2, CK)))
                        he_dma[hq] = em.c["q1"]
                    w1ms[e8] = em.c["sp"]
                em.w("q1", "q1", "sv")
                em.w("q2", "q1", "sv")
                for hf in range(2):
                    em.w("t", "sv", "sa")
                    first = True
                    pe_ms = {}
                    for e8 in range(EXP):
                        for f in range(NFC):
                            idx = e8 * NFC + f
                            gp = idx % 4
                            if idx >= 4:
                                em.wv("q1", "sp", pe_ms[idx - 4])
                                em.wv("q2", "sp", pe_ms[idx - 4])
                            em.q1(lambda e, e8=e8, f=f, hf=hf, gp=gp:
                                  e.dma_start(
                                      out=bA(e_w2s + gp * CK // 2, CK),
                                      in_=w2b[e8, f * 128:(f + 1) * 128,
                                              hf * 512:(hf + 1) * 512]))
                            em.q2(lambda e, e8=e8, f=f, gp=gp: e.dma_start(
                                out=bA(e_hrt + gp * CK // 2, CK),
                                in_=hed[e8, f * 128:(f + 1) * 128, :]))
                            em.wv("t", "q1", em.c["q1"])
                            em.wv("t", "q2", em.c["q2"])
                            for oi in range(4):
                                em.t(lambda e, oi=oi, gp=gp,
                                     first=first: e.matmul(
                                         pm[oi].ap()[:, :],
                                         bA(e_w2s + gp * CK // 2, CK)[
                                             :, oi * 128:(oi + 1) * 128],
                                         bA(e_hrt + gp * CK // 2, CK),
                                         start=first, stop=False),
                                     inc="sp" if oi == 3 else None)
                            pe_ms[idx] = em.c["sp"]
                            first = False
                    for oi in range(4):
                        d = hf * 4 + oi
                        em.t(lambda e, oi=oi, d=d: e.matmul(
                            pm[oi].ap()[:, :], b2bf[:, d * 128:(d + 1) * 128],
                            wfm[:, :], start=False, stop=True),
                            inc="sp" if oi == 3 else None)
                    em.w("v", "sp")
                    for oi in range(4):
                        d = hf * 4 + oi
                        em.v(lambda e, d=d, oi=oi: e.tensor_add(
                            fA(e_m2 + d * CK, CK), fA(e_x1 + d * CK, CK),
                            pm[oi].ap()[:, :]),
                            inc="sv" if oi == 3 else None)
                ln_stats(e_m2, e_sq, NCH)
                row_stats_close(e_mur, e_rsr, invH, e_row, e_lnt)
                em.w("v", "q2")
                for k in range(NCH):
                    em.v(lambda e, k=k: e.tensor_sub(
                        fA(e_lnt, CK), fA(e_m2 + k * CK, CK), fA(e_mur, CK)))
                    em.v(lambda e, k=k: e.scalar_tensor_tensor(
                        fA(e_fs, CK), fA(e_lnt, CK), ccol("gam", k),
                        fA(e_rsr, CK), ALU.mult, ALU.mult))
                    em.v(lambda e, k=k: e.tensor_scalar(
                        fA(e_fs, CK), fA(e_fs, CK), ccol("bet", k), None,
                        ALU.add))
                    em.v(lambda e, k=k: e.tensor_add(
                        fA(e_x1 + k * CK, CK), fA(e_x1 + k * CK, CK),
                        fA(e_fs, CK)),
                        inc="sv" if k == NCH - 1 else None)
                ln_stats(e_x1, e_sq, NCH)
                row_stats_close(e_mur, e_rsr, invH, e_row, e_lnt)
                em.w("v", "q2")
                for k in range(NCH):
                    em.v(lambda e, k=k: e.tensor_sub(
                        fA(e_lnt, CK), fA(e_x1 + k * CK, CK), fA(e_mur, CK)))
                    em.v(lambda e, k=k: e.tensor_mul(
                        fA(e_x1 + k * CK, CK), fA(e_lnt, CK), fA(e_rsr, CK)),
                        inc="sv" if k == NCH - 1 else None)
                em.w("q1", "sv")
                for k in range(NCH):
                    em.q1(lambda e, k=k, sl=sl: e.dma_start(
                        out=outt[k * 128:(k + 1) * 128, sl],
                        in_=fA(e_x1 + k * CK, CK)))
            em.w("q1", "q1")
            em.w("q1", "q2")

        with nc.Block() as block:
            @block.sync
            def _(eng):
                sched(Em("q1", eng))

            @block.gpsimd
            def _(eng):
                sched(Em("q2", eng))

            @block.tensor
            def _(eng):
                sched(Em("t", eng))

            @block.vector
            def _(eng):
                sched(Em("v", eng))

            @block.scalar
            def _(eng):
                sched(Em("s", eng))

    return nc


def kernel(**inp):
    hid = np.asarray(inp["L0_attn_hidden_states"], np.float32)
    wiT = np.ascontiguousarray(
        np.asarray(inp["L0_attn_in_proj_w"], np.float32).transpose(0, 2, 1))
    woT = np.ascontiguousarray(
        np.asarray(inp["L0_attn_out_proj_w"], np.float32).transpose(0, 2, 1))
    xp = np.asarray(inp["L0_attn_x_proj_w"], np.float32)
    wxp = np.zeros((L, DI, 128), np.float32)
    wxp[:, :, 0] = xp[:, 0, :]
    wxp[:, :, 32:48] = xp[:, 1:17, :].transpose(0, 2, 1)
    wxp[:, :, 64:80] = xp[:, 17:33, :].transpose(0, 2, 1)
    o1T = np.ascontiguousarray(np.asarray(inp["L0_attn_op1_w"], np.float32).T)
    o2T = np.ascontiguousarray(np.asarray(inp["L0_attn_op2_w"], np.float32).T)
    w1 = np.ascontiguousarray(np.asarray(inp["L0_ffn_W1"], np.float32))
    w2 = np.ascontiguousarray(np.asarray(inp["L0_ffn_W2"], np.float32))
    wg = np.ascontiguousarray(np.asarray(inp["L0_ffn_Wg"], np.float32))
    b2 = np.ascontiguousarray(np.asarray(inp["L0_ffn_b2"], np.float32))

    colp = np.zeros((128, NCOL), np.float32)

    def put(name, v):
        c = _cols(v)
        colp[:, COLL[name]:COLL[name] + c.shape[1]] = c

    put("inng", inp["L0_attn_in_norm_g"])
    put("innb", inp["L0_attn_in_norm_b"])
    put("lng", inp["L0_attn_ln_g"])
    put("lnb", inp["L0_attn_ln_b"])
    put("convw", np.asarray(inp["L0_attn_conv_w"],
                            np.float32).transpose(0, 1, 3, 2))
    put("convb", inp["L0_attn_conv_b"])
    put("dtw", np.asarray(inp["L0_attn_dt_proj_w"], np.float32)[:, :, 0])
    put("dtb", inp["L0_attn_dt_proj_b"])
    put("Dd", inp["L0_attn_D"])
    put("mlg", inp["L0_attn_mamba_ln_g"])
    put("mlb", inp["L0_attn_mamba_ln_b"])
    put("op1b", inp["L0_attn_op1_b"])
    put("op2b", inp["L0_attn_op2_b"])
    put("b1c", inp["L0_ffn_b1"])
    put("gam", inp["L0_ffn_gamma"])
    put("bet", inp["L0_ffn_beta"])

    bgr = np.ascontiguousarray(
        np.repeat(np.asarray(inp["L0_ffn_bg"], np.float32)[None, :], 128, 0))
    idb = np.eye(128, dtype=BF)
    _cv = [EPS, -0.5, -1.0, 1.0] + [-float(n) for n in range(2, 17)] + [0.0] * 5
    cstv = np.ascontiguousarray(np.repeat(np.array([_cv], np.float32), 128, 0))

    nc = build_nc()
    shared = dict(wi=wiT, wo=woT, wx=wxp, o1w=o1T, o2w=o2T, w1i=w1, w2i=w2,
                  wgi=wg, b2r=b2, colp=colp, bgr=bgr, idb=idb, cstv=cstv)
    in_maps = []
    for c in range(NCORE):
        m = dict(shared)
        m["x0t"] = np.ascontiguousarray(hid[c].T)
        in_maps.append(m)
    res = run_bass_kernel_spmd(nc, in_maps, core_ids=list(range(NCORE)))
    out = np.stack([np.asarray(r["outt"]).T for r in res.results], 0)
    return np.ascontiguousarray(out.astype(np.float32))
